# version 25
# baseline (speedup 1.0000x reference)
import contextlib
import numpy as np
import concourse.bass as bass
import concourse.mybir as mybir
from concourse.bass_utils import run_bass_kernel_spmd

F32 = mybir.dt.float32
BF16 = mybir.dt.bfloat16
AF = mybir.ActivationFunctionType
ALU = mybir.AluOpType
AX = mybir.AxisListType.X

P = 128
DM = 4096
NCORE = 8
TOK = 1024
NSUB = 4
NGRP = 2
OSL = 256
N_OSL = DM // OSL
ESL = 512
N_ESL = DM // ESL

ENGINES = ("tensor", "scalar", "vector", "gpsimd", "sync")

ST_SUM0, ST_SSQ0, ST_M1, ST_TMP, ST_VAR, ST_SD, ST_RS1, ST_NEGMS1 = range(8)
ST_DEN, ST_DINV = 8, 9
ST_S2, ST_SSQ2, ST_M2, ST_RS2, ST_NEGMS2 = 10, 11, 12, 13, 14
ST_SSQ3, ST_RS3, ST_NEGMS3 = 15, 16, 17
ST_S4, ST_SSQ4, ST_M4, ST_RS4, ST_NEGMS4 = 18, 19, 20, 21, 22
ST_EPS5, ST_EPS6 = 24, 25
ST_DEN8, ST_DINV8 = 96, 104
ST_S2P, ST_SSQ2P = 32, 40
ST_S4P, ST_SSQ4P = 64, 80


class Plan:
    def __init__(self):
        self.ops = {e: [] for e in ENGINES}
        import collections
        self.ctr = collections.defaultdict(int)
        self.pend = {e: {} for e in ENGINES}

    def wait(self, eng, sem, val):
        if val and val > self.pend[eng].get(sem, 0):
            self.pend[eng][sem] = val

    def _flush(self, eng):
        for sem, val in sorted(self.pend[eng].items()):
            self.ops[eng].append(("wait", sem, val))
        self.pend[eng] = {}

    OWN = {"scalar": "sa", "vector": "sv"}

    def op(self, eng, fn, sem=None, inc=0):
        own = self.OWN.get(eng)
        if own is not None and self.ctr[own]:
            self.wait(eng, own, self.ctr[own])
        self._flush(eng)
        self.ops[eng].append(("op", fn, sem, inc))
        if sem is not None:
            self.ctr[sem] += inc
            return self.ctr[sem]
        return None

    def dma(self, eng, fn, sem):
        return self.op(eng, fn, sem, 16)


class Tracker:

    def __init__(self, plan):
        self.p = plan
        self.st = {}

    def _overlaps(self, reg):
        buf, lo, hi = reg
        out = []
        for k in self.st:
            if k[0] == buf and k[1] < hi and lo < k[2]:
                out.append(k)
        return out

    def br(self, eng, reg):
        for k in self._overlaps(reg):
            w = self.st[k]["w"]
            if w:
                self.p.wait(eng, w[0], w[1])

    def ar(self, reg, sem, val):
        s = self.st.setdefault(reg, {"w": None, "r": {}})
        if s["r"].get(sem, 0) < val:
            s["r"][sem] = val

    def bw(self, eng, reg):
        for k in self._overlaps(reg):
            w = self.st[k]["w"]
            if w:
                self.p.wait(eng, w[0], w[1])
            for sem, val in self.st[k]["r"].items():
                self.p.wait(eng, sem, val)

    def aw(self, reg, sem, val):
        buf, lo, hi = reg
        for k in self._overlaps(reg):
            if lo <= k[1] and k[2] <= hi and k != reg:
                del self.st[k]
        self.st[reg] = {"w": (sem, val), "r": {}}


def build_nc():
    nc = bass.Bass("TRN2", target_bir_lowering=False)

    T = {}
    T["x_in"] = nc.dram_tensor("x", [TOK, DM], F32, kind="ExternalInput").ap()
    T["wqt"] = nc.dram_tensor("wqt", [DM, 1024], F32, kind="ExternalInput").ap()
    T["wot"] = nc.dram_tensor("wot", [DM, DM], F32, kind="ExternalInput").ap()
    T["wt"] = nc.dram_tensor("wt", [DM, DM], F32, kind="ExternalInput").ap()
    T["at_h"] = nc.dram_tensor("at_h", [DM, 128], F32, kind="ExternalInput").ap()
    T["bt_h"] = nc.dram_tensor("bt_h", [128, DM], F32, kind="ExternalInput").ap()
    T["m_in"] = nc.dram_tensor("m_in", [8, 128, 128], F32, kind="ExternalInput").ap()
    T["z_in"] = nc.dram_tensor("z_in", [8, 128], F32, kind="ExternalInput").ap()
    T["bq_in"] = nc.dram_tensor("bq_in", [1024], F32, kind="ExternalInput").ap()
    T["bias_in"] = nc.dram_tensor("bias_in", [DM], F32, kind="ExternalInput").ap()
    T["lw_in"] = nc.dram_tensor("lw_in", [8], F32, kind="ExternalInput").ap()
    T["eye_in"] = nc.dram_tensor("eye_in", [P, P], F32, kind="ExternalInput").ap()
    T["e8_in"] = nc.dram_tensor("e8_in", [8, P], F32, kind="ExternalInput").ap()
    T["y_out"] = nc.dram_tensor("y", [TOK, DM], F32, kind="ExternalOutput").ap()
    T["woeff_d"] = nc.dram_tensor("woeff_d", [1024, DM], BF16, kind="Internal").ap()
    T["wqb_d"] = nc.dram_tensor("wqb_d", [DM, 1024], BF16, kind="Internal").ap()
    T["wob_d"] = nc.dram_tensor("wob_d", [DM, DM], BF16, kind="Internal").ap()

    with contextlib.ExitStack() as ctx:
        E = ctx.enter_context
        T["XBUF"] = E(nc.sbuf_tensor("XBUF", [P, 4 * DM], F32))
        T["BIGA"] = E(nc.sbuf_tensor("BIGA", [P, 4 * DM], BF16))
        T["BBUF"] = E(nc.sbuf_tensor("BBUF", [P, 4 * DM], BF16))
        T["WSTR"] = E(nc.sbuf_tensor("WSTR", [P, 4 * DM], BF16))
        T["WSTG"] = E(nc.sbuf_tensor("WSTG", [P, 4 * DM], BF16))
        T["MZ"] = E(nc.sbuf_tensor("MZ", [P, 8 * 130], BF16))
        T["ROWS"] = E(nc.sbuf_tensor("ROWS", [1, 1024 + 128 + 2 * OSL], BF16))
        T["STATS"] = E(nc.sbuf_tensor("STATS", [P, 4 * 128], F32))
        T["EYEB"] = E(nc.sbuf_tensor("EYEB", [P, P], BF16))
        T["E8S"] = E(nc.sbuf_tensor("E8S", [8, P], F32))
        T["ZSTB"] = E(nc.sbuf_tensor("ZSTB", [8, P], BF16))
        T["E8B"] = E(nc.sbuf_tensor("E8B", [8, P], BF16))
        T["WVS"] = E(nc.sbuf_tensor("WVS", [8, 1], F32))
        T["COEF"] = E(nc.sbuf_tensor("COEF", [P, 1], F32))
        T["PS"] = E(nc.psum_tensor("PS", [P, 8 * 512], F32))
        sems = {n: E(nc.semaphore(n)) for n in (
            "sp", "sa", "sv", "dgc", "wq0", "wq1", "wq2", "wq3", "wo0", "wo1", "wt0", "wt1",
            "br0", "br1", "wos0", "wos1", "wos2", "wos3", "wost", "wop0", "wop1", "wqb", "stg", "atb", "sg",
            "xl0", "xl1", "xl2", "xl3", "xg0", "xg1", "xg2", "xg3")}
        block = E(nc.Block())

        plan = Plan()
        trk = Tracker(plan)
        Emit(plan, trk, T).program()

        def replay(name):
            def body(eng):
                for item in plan.ops[name]:
                    if item[0] == "wait":
                        eng.wait_ge(sems[item[1]], item[2])
                    else:
                        _, fn, sem, inc = item
                        ins = fn(eng)
                        if sem is not None:
                            ins.then_inc(sems[sem], inc)
            return body

        block.tensor(replay("tensor"))
        block.scalar(replay("scalar"))
        block.vector(replay("vector"))
        block.gpsimd(replay("gpsimd"))
        block.sync(replay("sync"))
    return nc


class Emit:
    def __init__(self, plan, trk, T):
        self.p = plan
        self.t = trk
        self.T = T
        self.x0_ms = {}
        self.store_ms = {}
        self.ones_ms = 0

    def Rps(self, b):
        return ("PS", b, b + 1)

    def Rx(self, s):
        return ("XB", s, s + 1)

    def Rbb(self, s):
        return ("BB", s, s + 1)

    def Rga(self, s):
        return ("GA", s, s + 1)

    def Rwq(self, par):
        return ("WSTR", par * 1024, (par + 1) * 1024)

    def Rretstg(self):
        return ("WSTR", 13312, 14336)

    def Rrett(self, s):
        return ("WSTR", 8192 + s * 1024, 8192 + (s + 1) * 1024)

    def Rqsel(self):
        return ("WSTR", 12288, 13312)

    def Rwo(self, par):
        return ("WSTR", par * DM, (par + 1) * DM)

    def Rwt(self, par):
        return ("WSTR", par * 8192, (par + 1) * 8192)

    def Rwoev(self):
        return ("WSTR", 0, DM)

    def Rat(self):
        return ("WSTG", 0, DM)

    def Rbt(self):
        return ("WSTG", DM, 2 * DM)

    def Rbtstg(self):
        return ("WSTG", 2 * DM, 3 * DM)

    def Rwostg(self, par):
        return ("WSTG", (2 + par) * DM, (3 + par) * DM)

    def Rdump(self):
        return ("WSTG", 3 * DM + 512, 3 * DM + 1024)

    def Rhtt(self, s):
        return ("WSTG", 3 * DM + 1024 + s * 128, 3 * DM + 1024 + (s + 1) * 128)

    def Rhb(self):
        return ("WSTG", 3 * DM + 1536, 3 * DM + 1664)

    def Rdump2(self):
        return ("WSTG", 3 * DM + 2048, 3 * DM + 2048 + OSL)

    def Rbias(self, par):
        return ("ROWS", 1152 + par * OSL, 1152 + (par + 1) * OSL)

    def st(self, s, slot, n=1):
        return self.T["STATS"][:, s * 128 + slot: s * 128 + slot + n]

    def bank(self, b, off=0, width=512):
        return self.T["PS"][:, b * 512 + off: b * 512 + off + width]

    def xsub(self, s):
        return self.T["XBUF"][:, s * DM:(s + 1) * DM]

    def bsub(self, s, a=0, b=DM):
        return self.T["BBUF"][:, s * DM + a: s * DM + b]

    def gsub(self, s, a=0, b=DM):
        return self.T["BIGA"][:, s * DM + a: s * DM + b]

    def wstr(self, a, b):
        return self.T["WSTR"][:, a:b]

    def wstg(self, a, b):
        return self.T["WSTG"][:, a:b]

    def ones(self):
        return self.T["ROWS"][0:1, 1024:1152][0:1, 0:128]

    def bqr(self, h):
        return self.T["ROWS"][0:1, h * 512:(h + 1) * 512]

    def biasr(self, par):
        return self.T["ROWS"][0:1, 1152 + par * OSL:1152 + (par + 1) * OSL]

    def evac(self, k, dst_ap_fn, bank_b, bank_off, width, dst_reg, pe_ms):
        eng = "scalar" if k % 2 == 0 else "vector"
        sem = "sa" if k % 2 == 0 else "sv"
        p, t = self.p, self.t
        p.wait(eng, "sp", pe_ms)
        t.br(eng, self.Rps(bank_b))
        t.bw(eng, dst_reg)
        if eng == "scalar":
            ms = p.op(eng, lambda e: e.activation(
                dst_ap_fn(), self.bank(bank_b, bank_off, width), AF.Copy), sem, 1)
        else:
            ms = p.op(eng, lambda e: e.tensor_copy(
                dst_ap_fn(), self.bank(bank_b, bank_off, width)), sem, 1)
        t.ar(self.Rps(bank_b), sem, ms)
        t.aw(dst_reg, sem, ms)
        return ms

    def trans_block(self, grp, srcs, dst_fn, banks2, src_reg, dst_reg,
                    eng_force=None):
        p, t = self.p, self.t
        b = banks2[grp % 2]
        t.bw("tensor", self.Rps(b))
        t.br("tensor", src_reg)
        for k in range(4):
            pe = p.op("tensor", lambda e, k=k, b=b, f=srcs[k]: e.matmul(
                self.bank(b, k * 128, 128), f(), self.T["EYEB"][:],
                start=True, stop=True), "sp", 1 if k == 3 else 0)
        t.aw(self.Rps(b), "sp", pe)
        t.ar(src_reg, "sp", pe)
        eng = eng_force or ("scalar" if grp % 2 == 0 else "vector")
        sem = "sa" if eng == "scalar" else "sv"
        p.wait(eng, "sp", pe)
        t.br(eng, self.Rps(b))
        t.bw(eng, dst_reg)
        if eng == "scalar":
            ms = p.op(eng, lambda e, b=b: e.activation(
                dst_fn(), self.bank(b, 0, 512), AF.Copy), sem, 1)
        else:
            ms = p.op(eng, lambda e, b=b: e.tensor_copy(
                dst_fn(), self.bank(b, 0, 512)), sem, 1)
        t.ar(self.Rps(b), sem, ms)
        t.aw(dst_reg, sem, ms)
        return ms

    def finalize_stats(self, s, slot_sum, slot_ssq, slot_m, slot_rs,
                       slot_negms, eps, reuse_m=False):
        p = self.p
        st = self.st
        if not reuse_m:
            p.op("scalar", lambda e: e.activation(
                st(s, slot_m), st(s, slot_sum), AF.Copy, scale=1.0 / DM),
                "sa", 1)
        p.op("scalar", lambda e: e.activation(
            st(s, ST_TMP), st(s, slot_m), AF.Square), "sa", 1)
        p.op("scalar", lambda e: e.activation(
            st(s, ST_SD), st(s, ST_TMP), AF.Copy, scale=-1.0), "sa", 1)
        p.op("scalar", lambda e: e.activation(
            st(s, ST_VAR), st(s, slot_ssq), AF.Identity, scale=1.0 / DM,
            bias=st(s, ST_SD)), "sa", 1)
        p.op("scalar", lambda e: e.activation(
            st(s, ST_TMP), st(s, ST_VAR), AF.Ln, bias=st(s, ST_EPS5)), "sa", 1)
        p.op("scalar", lambda e: e.activation(
            st(s, slot_rs), st(s, ST_TMP), AF.Exp, scale=-0.5), "sa", 1)
        p.op("scalar", lambda e: e.activation(
            st(s, ST_DEN), st(s, slot_m), AF.Copy, scale=-1.0), "sa", 1)
        sa = p.op("scalar", lambda e: e.activation(
            st(s, slot_negms), st(s, ST_DEN), AF.Identity,
            scale=st(s, slot_rs)), "sa", 1)
        return sa

    def program(self):
        p, t, T = self.p, self.t, self.T

        p.dma("gpsimd", lambda e: e.dma_start(T["EYEB"][:], T["eye_in"]), "dgc")
        p.dma("gpsimd", lambda e: e.dma_start(T["E8S"][:], T["e8_in"]), "dgc")
        p.dma("gpsimd", lambda e: e.dma_start(
            T["WVS"][:], T["lw_in"].rearrange("(p a) -> p a", a=1)), "dgc")
        for j in range(8):
            p.dma("gpsimd", lambda e, j=j: e.dma_start(
                T["MZ"][:, j * 130:j * 130 + 128],
                bass.AP(T["m_in"].tensor, j * 128 * 128, [[128, 128], [1, 128]])),
                "stg")
        p.dma("gpsimd", lambda e: e.dma_start(T["ZSTB"][:], T["z_in"]), "stg")
        p.dma("gpsimd", lambda e: e.dma_start(T["E8B"][:], T["e8_in"]), "stg")
        p.dma("gpsimd", lambda e: e.dma_start(
            T["ROWS"][0:1, 0:1024],
            bass.AP(T["bq_in"].tensor, 0, [[1, 1], [1, 1024]])), "stg")
        self.ones_ms = p.op("vector", lambda e: e.memset(self.ones(), 1.0), "sv", 1)
        for s in range(NSUB):
            p.op("vector", lambda e, s=s: e.memset(self.st(s, ST_EPS5), 1e-5),
                 "sv", 1)
            self.eps_ms = p.op("vector", lambda e, s=s: e.memset(
                self.st(s, ST_EPS6), 1e-6), "sv", 1)

        t.bw("gpsimd", self.Rat())
        p.dma("gpsimd", lambda e: e.dma_start(
            self.wstg(0, DM).rearrange("p (c n) -> p c n", n=128),
            bass.AP(T["at_h"].tensor, 0, [[128, 128], [128 * 128, 32], [1, 128]])),
            "atb")
        t.bw("gpsimd", self.Rbtstg())
        dg_ab = p.dma("gpsimd", lambda e: e.dma_start(
            self.wstg(2 * DM, 3 * DM), T["bt_h"]), "atb")
        self.dgc_all = p.ctr["dgc"]
        self.stg_all = p.ctr["stg"]
        self.atb_all = p.ctr["atb"]
        t.aw(self.Rat(), "atb", self.atb_all)
        t.aw(self.Rbtstg(), "atb", self.atb_all)

        for c in range(4):
            wqc = p.dma("gpsimd", lambda e, c=c: e.dma_start(
                bass.AP(T["wqb_d"].tensor, c * 1024 * 1024, [[1024, 1024], [1, 1024]]),
                bass.AP(T["wqt"].tensor, c * 1024 * 1024, [[1024, 1024], [1, 1024]])),
                "wqb")
        self.wq_bounce_done = wqc



        self.main_loads()
        self.prep_queue = list(range(8))
        self.phase_A(0)
        self.prep_some(8)
        self.wo_prep_done = (p.ctr["wop0"], p.ctr["wop1"])
        self.phase_B(0)
        self.phase_F(0)
        for g in range(1, NGRP):
            self.phase_A(g)
            self.phase_B(g)
            self.phase_F(g)
        for s in range(NSUB):
            p.wait("sync", f"xl{s}", p.ctr[f"xl{s}"])
        p._flush("sync")

    def coef_bt(self):
        p, t, T = self.p, self.t, self.T

        p.wait("tensor", "dgc", self.dgc_all)
        t.bw("tensor", self.Rps(0))
        pe = p.op("tensor", lambda e: e.matmul(
            self.bank(0, 0, 1), T["E8S"][:], T["WVS"][:], start=True, stop=True),
            "sp", 1)
        t.aw(self.Rps(0), "sp", pe)
        p.wait("scalar", "sp", pe)
        sa = p.op("scalar", lambda e: e.activation(
            T["COEF"][:], self.bank(0, 0, 1), AF.Copy, scale=2.0), "sa", 1)
        t.ar(self.Rps(0), "sa", sa)

        p.wait("tensor", "stg", p.ctr["stg"])
        t.bw("tensor", self.Rps(1))
        pe = p.op("tensor", lambda e: e.matmul(
            self.bank(1, 0, 128), T["ZSTB"][:], T["E8B"][:],
            start=True, stop=True), "sp", 1)
        t.aw(self.Rps(1), "sp", pe)
        p.wait("scalar", "sp", pe)
        sa = p.op("scalar", lambda e: e.activation(
            T["MZ"][:].rearrange("p (j c) -> p j c", c=130)[:, :, 128:129],
            T["PS"][:, 512:640].rearrange("p (j c) -> p j c", c=16)[:, :, 0:1],
            AF.Copy), "sa", 1)
        t.ar(self.Rps(1), "sa", sa)
        self.zcols_ms = sa

        p.wait("vector", "sa", sa)
        p.wait("vector", "atb", self.atb_all)
        t.bw("vector", self.Rbt())
        t.br("vector", self.Rbtstg())
        sv = p.op("vector", lambda e: e.tensor_scalar_mul(
            self.wstg(DM, 2 * DM), self.wstg(2 * DM, 3 * DM), T["COEF"][:, 0:1]),
            "sv", 1)
        t.aw(self.Rbt(), "sv", sv)
        t.ar(self.Rbtstg(), "sv", sv)

    def prep_some(self, n):
        for _ in range(n):
            if self.prep_queue:
                self.wo_prep_j(self.prep_queue.pop(0))

    def wo_prep_j(self, j):
        p, t, T = self.p, self.t, self.T
        for eh in range(2):
            for g in range(4):
                slot = (2 * eh + g) % 4
                lo = 2 * DM + slot * 2048
                Rst = ("WSTG", lo, lo + 2048)
                t.bw("gpsimd", Rst)
                dg = p.dma("gpsimd", lambda e, j=j, g=g, eh=eh, lo=lo: e.dma_start(
                    self.wstg(lo, lo + 2048),
                    bass.AP(T["wot"].tensor,
                            (4 * j + g) * 128 * DM + eh * 2048,
                            [[DM, 128], [1, 2048]])), f"wos{slot}")
                t.aw(Rst, f"wos{slot}", dg)
                p.wait("tensor", f"wos{slot}", dg)
                t.br("tensor", Rst)
                if g == 0:
                    for o in range(4):
                        t.bw("tensor", self.Rps(4 + o))
                for o in range(4):
                    pe = p.op("tensor", lambda e, g=g, o=o, lo=lo: e.matmul(
                        self.bank(4 + o), T["EYEB"][:],
                        self.wstg(lo + o * 512, lo + (o + 1) * 512),
                        start=(g == 0), stop=(g == 3)),
                        "sp", 1 if o == 3 else 0)
                t.ar(Rst, "sp", pe)
            for o in range(4):
                t.aw(self.Rps(4 + o), "sp", pe)
            Rev = ("WSTR", eh * 2048, (eh + 1) * 2048)
            t.bw("scalar", Rev)
            p.wait("scalar", "sp", pe)
            for o in range(4):
                t.br("scalar", self.Rps(4 + o))
                sa = p.op("scalar", lambda e, o=o, eh=eh: e.activation(
                    self.wstr(eh * 2048 + o * 512, eh * 2048 + (o + 1) * 512),
                    self.bank(4 + o), AF.Copy), "sa", 1 if o == 3 else 0)
            for o in range(4):
                t.ar(self.Rps(4 + o), "sa", sa)
            t.aw(Rev, "sa", sa)
            t.br("sync", Rev)
            ds = p.dma("sync", lambda e, j=j, eh=eh: e.dma_start(
                bass.AP(T["woeff_d"].tensor, j * 128 * DM + eh * 2048,
                        [[DM, 128], [1, 2048]]),
                self.wstr(eh * 2048, (eh + 1) * 2048)), f"wop{eh}")
            t.ar(Rev, f"wop{eh}", ds)

    def main_loads(self):
        p, t, T = self.p, self.t, self.T
        for s in range(NSUB):
            t.bw("sync", self.Rx(s))
            ds = p.dma("sync", lambda e, s=s: e.dma_start(
                self.xsub(s), T["x_in"][s * 128:(s + 1) * 128, :]), f"xl{s}")
            t.aw(self.Rx(s), f"xl{s}", ds)
            self.x0_ms[(0, s)] = ds

    def a_stats(self, g, s):
        p, t, T = self.p, self.t, self.T
        st = self.st
        xsub, bsub = self.xsub, self.bsub
        xsem = f"xl{s}" if g == 0 else f"xg{s}"
        p.wait("vector", xsem, self.x0_ms[(g, s)])
        p.wait("scalar", xsem, self.x0_ms[(g, s)])
        if g == 0:
            t.br("vector", self.Rx(s))
            sv = p.op("vector", lambda e, s=s: e.reduce_sum(
                st(s, ST_SUM0), xsub(s), axis=AX), "sv", 1)
            t.ar(self.Rx(s), "sv", sv)
            self._a_stats_sv = sv
        else:
            t.bw("scalar", self.Rbb(s))
            t.br("scalar", self.Rx(s))
            sa = p.op("scalar", lambda e, s=s: e.activation(
                bsub(s), xsub(s), AF.Copy, accum_out=st(s, ST_SUM0)),
                "sa", 1)
            t.aw(self.Rbb(s), "sa", sa)
            t.ar(self.Rx(s), "sa", sa)
        t.bw("scalar", self.Rbb(s))
        t.br("scalar", self.Rx(s))
        sa = p.op("scalar", lambda e, s=s: e.activation(
            bsub(s), xsub(s), AF.Square, accum_out=st(s, ST_SSQ0)),
            "sa", 1)
        t.aw(self.Rbb(s), "sa", sa)
        t.ar(self.Rx(s), "sa", sa)
        self._a_stats_ms = sa

    def a_fin(self, g, s):
        p, t, T = self.p, self.t, self.T
        st = self.st
        xsub, bsub, gsub = self.xsub, self.bsub, self.gsub
        p.wait("scalar", "sa", self._a_stats_ms)
        p.wait("scalar", "sv", self._a_stats_sv)
        self.finalize_stats(s, ST_SUM0, ST_SSQ0, ST_M1, ST_RS1,
                            ST_NEGMS1, 1e-5)
        t.bw("scalar", self.Rbb(s))
        t.br("scalar", self.Rx(s))
        sa = p.op("scalar", lambda e, s=s: e.activation(
            bsub(s), xsub(s), AF.Identity, bias=st(s, ST_NEGMS1),
            scale=st(s, ST_RS1)), "sa", 1)
        t.aw(self.Rbb(s), "sa", sa)
        t.ar(self.Rx(s), "sa", sa)
        p.wait("tensor", "sa", sa)
        for G in range(8):
            self.trans_block(
                G,
                [(lambda s=s, fc=G * 4 + k: bsub(s, fc * 128, (fc + 1) * 128))
                 for k in range(4)],
                (lambda s=s, G=G: gsub(s, G * 512, (G + 1) * 512)),
                (0, 1), self.Rbb(s), self.Rga(s))

    def phase_A(self, g):
        p, t, T = self.p, self.t, self.T
        st, bank = self.st, self.bank
        xsub, bsub, gsub = self.xsub, self.bsub, self.gsub

        if g == 0:
            p.wait("tensor", "dgc", self.dgc_all)
            for s in range(NSUB):
                self.a_stats(g, s)
                self.a_fin(g, s)
            self.coef_bt()

        qdone = {}
        for fc in range(32):
            par = fc % 4
            t.bw("sync", self.Rwq(par))
            if fc < 4:
                p.wait("sync", "wqb", self.wq_bounce_done)
            dg = p.dma("sync", lambda e, fc=fc, par=par: e.dma_start(
                self.wstr(par * 1024, (par + 1) * 1024),
                bass.AP(T["wqb_d"].tensor, fc * 128 * 1024, [[1024, 128], [1, 1024]])),
                f"wq{par}")
            t.aw(self.Rwq(par), f"wq{par}", dg)
            p.wait("tensor", f"wq{par}", dg)
            t.br("tensor", self.Rwq(par))
            for s in range(NSUB):
                t.br("tensor", self.Rga(s))
                for h in range(2):
                    b = 2 * s + h
                    if fc == 0:
                        t.bw("tensor", self.Rps(b))
                    pe = p.op("tensor",
                              lambda e, s=s, h=h, fc=fc, b=b, par=par: e.matmul(
                                  bank(b), gsub(s, fc * 128, (fc + 1) * 128),
                                  self.wstr(par * 1024 + h * 512,
                                            par * 1024 + (h + 1) * 512),
                                  start=(fc == 0), stop=False), "sp", 1)
            t.ar(self.Rwq(par), "sp", pe)
            for s in range(NSUB):
                t.ar(self.Rga(s), "sp", pe)
        p.wait("tensor", "sv", self.ones_ms)
        p.wait("tensor", "stg", self.stg_all)
        p.wait("tensor", "sa", self.zcols_ms)
        for s in range(NSUB):
            for h in range(2):
                b = 2 * s + h
                pe = p.op("tensor", lambda e, h=h, b=b: e.matmul(
                    bank(b), self.ones(), self.bqr(h), start=False, stop=True),
                    "sp", 1)
                t.aw(self.Rps(b), "sp", pe)
                qdone[(s, h)] = pe

        sig_ms = {}
        for s in range(NSUB):
            for h in range(2):
                b = 2 * s + h
                pe = qdone[(s, h)]
                p.wait("scalar", "sp", pe)
                t.br("scalar", self.Rps(b))
                t.bw("scalar", self.Rbb(s))
                sa = p.op("scalar", lambda e, s=s, h=h, b=b: e.activation(
                    bsub(s, 1024 + h * 512, 1024 + (h + 1) * 512), bank(b),
                    AF.Relu), "sa", 1)
                t.ar(self.Rps(b), "sa", sa)
                t.aw(self.Rbb(s), "sa", sa)
                p.wait("vector", "sa", sa)
                p.wait("vector", "sp", pe)
                t.br("vector", self.Rps(b))
                t.bw("vector", self.Rbb(s))
                sv = p.op("vector", lambda e, s=s, h=h, b=b: e.tensor_tensor(
                    bsub(s, 2048 + h * 512, 2048 + (h + 1) * 512), bank(b),
                    bsub(s, 1024 + h * 512, 1024 + (h + 1) * 512), ALU.subtract),
                    "sv", 1)
                t.ar(self.Rps(b), "sv", sv)
                t.aw(self.Rbb(s), "sv", sv)
                p.wait("scalar", "sv", sv)
                sa = p.op("scalar", lambda e, s=s, h=h: e.activation(
                    bsub(s, 3072 + h * 512, 3072 + (h + 1) * 512),
                    bsub(s, 2048 + h * 512, 2048 + (h + 1) * 512), AF.Exp),
                    "sa", 1)
                p.wait("vector", "sa", sa)
                sv = p.op("vector", lambda e, s=s, h=h: e.tensor_tensor(
                    bsub(s, h * 512, (h + 1) * 512),
                    bsub(s, 1024 + h * 512, 1024 + (h + 1) * 512),
                    bsub(s, 3072 + h * 512, 3072 + (h + 1) * 512), ALU.add),
                    "sv", 1)
                t.aw(self.Rbb(s), "sv", sv)
            sig_ms[s] = sv
        if g == 0:
            self.prep_some(1)

        for s in range(NSUB):
            p.wait("tensor", "sv", sig_ms[s])
            for G in range(2):
                self.trans_block(
                    G,
                    [(lambda s=s, j=G * 4 + k: bsub(s, j * 128, (j + 1) * 128))
                     for k in range(4)],
                    (lambda G=G: self.wstr(12288 + G * 512, 12288 + (G + 1) * 512)),
                    (0, 1), self.Rbb(s), self.Rqsel())
            t.br("tensor", self.Rqsel())
            for r in range(2):
                for k in range(4):
                    j = r * 4 + k
                    t.bw("tensor", self.Rps(k))
                    pe = p.op("tensor", lambda e, j=j, k=k: e.matmul(
                        bank(k, 0, 129),
                        self.wstr(12288 + j * 128, 12288 + (j + 1) * 128),
                        T["MZ"][:, j * 130:j * 130 + 129], start=True, stop=True),
                        "sp", 1)
                    t.aw(self.Rps(k), "sp", pe)
                t.ar(self.Rqsel(), "sp", pe)
                p.wait("scalar", "sp", pe)
                for k in range(4):
                    j = r * 4 + k
                    sa = p.op("scalar", lambda e, s=s, j=j, k=k: e.activation(
                        st(s, ST_DEN8 + j), bank(k, 128, 1), AF.Identity,
                        bias=st(s, ST_EPS6)), "sa", 1)
                p.wait("vector", "sa", sa)
                sv = p.op("vector", lambda e, s=s, r=r: e.reciprocal(
                    st(s, ST_DINV8 + r * 4, 4), st(s, ST_DEN8 + r * 4, 4)),
                    "sv", 1)
                p.wait("scalar", "sv", sv)
                t.bw("scalar", self.Rretstg())
                t.bw("vector", self.Rretstg())
                for k in range(4):
                    j = r * 4 + k
                    if k % 2 == 0:
                        sa = p.op("scalar", lambda e, s=s, j=j, k=k: e.activation(
                            self.wstr(13312 + j * 128, 13312 + (j + 1) * 128),
                            bank(k, 0, 128), AF.Copy, scale=st(s, ST_DINV8 + j)),
                            "sa", 1)
                        t.ar(self.Rps(k), "sa", sa)
                    else:
                        svr = p.op("vector", lambda e, s=s, j=j, k=k: e.tensor_scalar_mul(
                            self.wstr(13312 + j * 128, 13312 + (j + 1) * 128),
                            bank(k, 0, 128), st(s, ST_DINV8 + j)), "sv", 1)
                        t.ar(self.Rps(k), "sv", svr)
                t.aw(self.Rretstg(), "sa", sa)
                t.ar(self.Rretstg(), "sv", svr)
                p.wait("tensor", "sv", svr)
                p.wait("tensor", "sa", sa)
                self.trans_block(
                    r,
                    [(lambda j=r * 4 + k: self.wstr(13312 + j * 128,
                                                    13312 + (j + 1) * 128))
                     for k in range(4)],
                    (lambda s=s, r=r: self.wstr(8192 + s * 1024 + r * 512,
                                                8192 + s * 1024 + (r + 1) * 512)),
                    (2, 3), self.Rretstg(), self.Rrett(s))
            if g == 0:
                self.prep_some(1 if s < 3 else 2)

    def phase_B(self, g):
        p, t, T = self.p, self.t, self.T
        st, bank = self.st, self.bank
        xsub, bsub, gsub = self.xsub, self.bsub, self.gsub
        self.b_evac_sub = {}
        self.htt_ms = {}

        for i in range(N_ESL):
            par = i % 2
            t.bw("sync", self.Rwo(par))
            if g == 0 and i < 2:
                p.wait("sync", "wop0", self.wo_prep_done[0])
                p.wait("sync", "wop1", self.wo_prep_done[1])
            ds = p.dma("sync", lambda e, i=i, par=par: e.dma_start(
                self.wstr(par * DM, (par + 1) * DM).rearrange(
                    "p (j x) -> p j x", x=512),
                bass.AP(T["woeff_d"].tensor, i * 512,
                        [[DM, 128], [128 * DM, 8], [1, 512]])), f"wo{par}")
            t.aw(self.Rwo(par), f"wo{par}", ds)
            p.wait("tensor", f"wo{par}", ds)
            t.br("tensor", self.Rwo(par))
            self.b_slice(g, i, par, (0, 1, 2, 3))

    def b_slice(self, g, i, par, subs):
        p, t, T = self.p, self.t, self.T
        st, bank = self.st, self.bank
        bsub = self.bsub
        if True:
            for s in subs:
                b = par * 4 + s
                t.bw("tensor", self.Rps(b))
                t.br("tensor", self.Rrett(s))
                for j in range(8):
                    pe = p.op("tensor",
                              lambda e, s=s, j=j, b=b, par=par: e.matmul(
                                  bank(b),
                                  self.wstr(8192 + s * 1024 + j * 128,
                                            8192 + s * 1024 + (j + 1) * 128),
                                  self.wstr(par * DM + j * 512,
                                            par * DM + (j + 1) * 512),
                                  start=(j == 0), stop=(j == 7)),
                              "sp", 1 if j == 7 else 0)
                t.aw(self.Rps(b), "sp", pe)
                t.ar(self.Rrett(s), "sp", pe)
                t.ar(self.Rwo(par), "sp", pe)
                p.wait("scalar", "sp", pe)
                t.br("scalar", self.Rps(b))
                t.bw("scalar", self.Rbb(s))
                sa = p.op("scalar", lambda e, s=s, i=i, b=b: e.activation(
                    bsub(s, i * 512, (i + 1) * 512), bank(b), AF.Copy,
                    accum_out=st(s, ST_S2P + i)), "sa", 1)
                t.aw(self.Rbb(s), "sa", sa)
                t.bw("scalar", self.Rdump())
                p.wait("vector", "sa", sa)
                t.bw("vector", self.Rdump())
                svq = p.op("vector", lambda e, s=s, i=i: e.scalar_tensor_tensor(
                    self.wstg(3 * DM + 512, 3 * DM + 1024),
                    bsub(s, i * 512, (i + 1) * 512), 1.0,
                    bsub(s, i * 512, (i + 1) * 512), ALU.mult, ALU.mult,
                    accum_out=st(s, ST_SSQ2P + i)), "sv", 1)
                t.aw(self.Rdump(), "sv", svq)
                t.ar(self.Rps(b), "sa", sa)
                self.b_evac_sub[s] = (sa, svq)

    def fp_reduce(self, s):
        p, st = self.p, self.st
        p.wait("vector", "sa", self.b_evac_sub[s][0])
        p.wait("vector", "sv", self.b_evac_sub[s][1])
        p.op("vector", lambda e, s=s: e.reduce_sum(
            st(s, ST_S2), st(s, ST_S2P, 8), axis=AX), "sv", 1)
        sv = p.op("vector", lambda e, s=s: e.reduce_sum(
            st(s, ST_SSQ2), st(s, ST_SSQ2P, 8), axis=AX), "sv", 1)
        self._fp_red[s] = sv

    def fp_fin2(self, s):
        p = self.p
        p.wait("scalar", "sv", self._fp_red[s])
        self._fp_fin2[s] = self.finalize_stats(
            s, ST_S2, ST_SSQ2, ST_M2, ST_RS2, ST_NEGMS2, 1e-5)

    def fp_resid(self, s):
        p, t = self.p, self.t
        st, xsub, bsub, gsub = self.st, self.xsub, self.bsub, self.gsub
        p.wait("vector", "sa", self._fp_fin2[s])
        t.bw("vector", self.Rx(s))
        t.br("vector", self.Rbb(s))
        p.op("vector", lambda e, s=s: e.tensor_scalar(
            xsub(s), xsub(s), st(s, ST_NEGMS2), None, ALU.add), "sv", 1)
        sv = p.op("vector", lambda e, s=s: e.scalar_tensor_tensor(
            xsub(s), bsub(s), st(s, ST_RS2), xsub(s), ALU.mult, ALU.add),
            "sv", 1)
        t.aw(self.Rx(s), "sv", sv)
        t.ar(self.Rbb(s), "sv", sv)
        t.bw("vector", self.Rga(s))
        sv3 = p.op("vector", lambda e, s=s: e.scalar_tensor_tensor(
            gsub(s), xsub(s), 1.0, xsub(s), ALU.mult, ALU.mult,
            accum_out=st(s, ST_SSQ3)), "sv", 1)
        t.aw(self.Rga(s), "sv", sv3)
        t.ar(self.Rx(s), "sv", sv3)
        self._fp_ssq3[s] = sv3

    def f_pre(self, g, s):
        p, t, T = self.p, self.t, self.T
        st, bank = self.st, self.bank
        xsub, bsub, gsub = self.xsub, self.bsub, self.gsub
        if True:
            p.wait("scalar", "sv", self._fp_ssq3[s])
            self.finalize_stats(s, ST_SUM0, ST_SSQ3, ST_M1, ST_RS3,
                                 ST_NEGMS3, 1e-5, reuse_m=True)
            t.bw("scalar", self.Rbb(s))
            t.br("scalar", self.Rx(s))
            sa = p.op("scalar", lambda e, s=s: e.activation(
                bsub(s), xsub(s), AF.Identity, bias=st(s, ST_NEGMS3),
                scale=st(s, ST_RS3)), "sa", 1)
            t.aw(self.Rbb(s), "sa", sa)
            t.ar(self.Rx(s), "sa", sa)
            p.wait("tensor", "sa", sa)
            for G in range(8):
                self.trans_block(
                    G,
                    [(lambda s=s, fc=G * 4 + k: bsub(s, fc * 128, (fc + 1) * 128))
                     for k in range(4)],
                    (lambda s=s, G=G: gsub(s, G * 512, (G + 1) * 512)),
                    (0, 1), self.Rbb(s), self.Rga(s), eng_force="scalar")
            b = 2 + (s % 2)
            t.bw("tensor", self.Rps(b))
            t.br("tensor", self.Rga(s))
            t.br("tensor", self.Rat())
            for fc in range(32):
                pe = p.op("tensor", lambda e, s=s, fc=fc, b=b: e.matmul(
                    bank(b, 0, 128), gsub(s, fc * 128, (fc + 1) * 128),
                    self.wstg(fc * 128, (fc + 1) * 128),
                    start=(fc == 0), stop=(fc == 31)), "sp", 1 if fc == 31 else 0)
            t.aw(self.Rps(b), "sp", pe)
            t.ar(self.Rga(s), "sp", pe)
            t.ar(self.Rat(), "sp", pe)
            p.wait("scalar", "sp", pe)
            t.br("scalar", self.Rps(b))
            t.bw("scalar", self.Rhb())
            sa = p.op("scalar", lambda e, b=b: e.activation(
                self.wstg(3 * DM + 1536, 3 * DM + 1664), bank(b, 0, 128),
                AF.Copy), "sa", 1)
            t.ar(self.Rps(b), "sa", sa)
            t.aw(self.Rhb(), "sa", sa)
            p.wait("tensor", "sa", sa)
            t.bw("tensor", self.Rps(b))
            t.br("tensor", self.Rhb())
            pe = p.op("tensor", lambda e, b=b: e.matmul(
                bank(b, 384, 128), self.wstg(3 * DM + 1536, 3 * DM + 1664),
                T["EYEB"][:], start=True, stop=True), "sp", 1)
            t.aw(self.Rps(b), "sp", pe)
            t.ar(self.Rhb(), "sp", pe)
            p.wait("scalar", "sp", pe)
            t.br("scalar", self.Rps(b))
            t.bw("scalar", self.Rhtt(s))
            sa = p.op("scalar", lambda e, s=s, b=b: e.activation(
                self.wstg(3 * DM + 1024 + s * 128, 3 * DM + 1024 + (s + 1) * 128),
                bank(b, 384, 128), AF.Copy), "sa", 1)
            t.ar(self.Rps(b), "sa", sa)
            t.aw(self.Rhtt(s), "sa", sa)
            self.htt_ms[s] = sa

    def phase_F(self, g):
        p, t, T = self.p, self.t, self.T
        st, bank = self.st, self.bank
        xsub, bsub, gsub = self.xsub, self.bsub, self.gsub
        self._fp_red, self._fp_fin2, self._fp_ssq3 = {}, {}, {}
        for s in range(NSUB):
            self.fp_reduce(s)
        for s in range(NSUB):
            self.fp_fin2(s)
        for s in range(NSUB):
            self.fp_resid(s)
        for s in range(NSUB):
            self.f_pre(g, s)
        htt_ms = self.htt_ms

        self.f_evac_sub = {}
        for i in range(N_OSL):
            par = i % 2
            t.bw("gpsimd", self.Rbias(par))
            dgb = p.dma("gpsimd", lambda e, i=i, par=par: e.dma_start(
                self.biasr(par),
                bass.AP(T["bias_in"].tensor, i * OSL, [[1, 1], [1, OSL]])),
                f"br{par}")
            t.aw(self.Rbias(par), f"br{par}", dgb)
            t.bw("gpsimd", self.Rwt(par))
            for q in range(4):
                dg = p.dma("gpsimd", lambda e, i=i, q=q, par=par: e.dma_start(
                    self.wstr(par * 8192 + q * 2048,
                              par * 8192 + (q + 1) * 2048).rearrange(
                        "p (c o) -> p c o", o=OSL),
                    bass.AP(T["wt"].tensor, (8 * q * 128) * DM + i * OSL,
                            [[DM, 128], [128 * DM, 8], [1, OSL]])), f"wt{par}")
            t.aw(self.Rwt(par), f"wt{par}", dg)
            p.wait("tensor", f"wt{par}", dg)
            p.wait("tensor", f"br{par}", dgb)
            t.br("tensor", self.Rwt(par))
            t.br("tensor", self.Rbias(par))
            t.br("tensor", self.Rbt())
            chain_pe = {}
            for s in range(NSUB):
                b = par * 2 + s // 2
                off = (s % 2) * 256
                if s % 2 == 0:
                    t.bw("tensor", self.Rps(b))
                t.br("tensor", self.Rga(s))
                for fc in range(32):
                    p.op("tensor",
                         lambda e, s=s, fc=fc, b=b, off=off, par=par: e.matmul(
                             bank(b, off, OSL), gsub(s, fc * 128, (fc + 1) * 128),
                             self.wstr(par * 8192 + fc * OSL,
                                       par * 8192 + (fc + 1) * OSL),
                             start=(fc == 0), stop=False), "sp", 0)
                p.wait("tensor", "sa", htt_ms[s])
                t.br("tensor", self.Rhtt(s))
                p.op("tensor", lambda e, s=s, b=b, off=off, i=i: e.matmul(
                    bank(b, off, OSL),
                    self.wstg(3 * DM + 1024 + s * 128, 3 * DM + 1024 + (s + 1) * 128),
                    self.wstg(DM + i * OSL, DM + (i + 1) * OSL),
                    start=False, stop=False), "sp", 0)
                pe = p.op("tensor", lambda e, b=b, off=off, par=par: e.matmul(
                    bank(b, off, OSL), self.ones(), self.biasr(par),
                    start=False, stop=True), "sp", 1)
                chain_pe[s] = pe
                t.aw(self.Rps(b), "sp", pe)
                t.ar(self.Rga(s), "sp", pe)
                t.ar(self.Rwt(par), "sp", pe)
                t.ar(self.Rbias(par), "sp", pe)
                t.ar(self.Rbt(), "sp", pe)
                t.ar(self.Rhtt(s), "sp", pe)
            for s in range(NSUB):
                b = par * 2 + s // 2
                off = (s % 2) * 256
                p.wait("scalar", "sp", max(chain_pe[s], chain_pe[s ^ 1]))
                t.br("scalar", self.Rps(b))
                t.bw("scalar", self.Rbb(s))
                sa = p.op("scalar", lambda e, s=s, i=i, b=b, off=off: e.activation(
                    bsub(s, i * OSL, (i + 1) * OSL), bank(b, off, OSL), AF.Copy,
                    accum_out=st(s, ST_S4P + i)), "sa", 1)
                t.aw(self.Rbb(s), "sa", sa)
                p.wait("vector", "sa", sa)
                t.bw("vector", self.Rdump2())
                svq = p.op("vector", lambda e, s=s, i=i: e.scalar_tensor_tensor(
                    self.wstg(3 * DM + 2048, 3 * DM + 2048 + OSL),
                    bsub(s, i * OSL, (i + 1) * OSL), 1.0,
                    bsub(s, i * OSL, (i + 1) * OSL), ALU.mult, ALU.mult,
                    accum_out=st(s, ST_SSQ4P + i)), "sv", 1)
                t.aw(self.Rdump2(), "sv", svq)
                t.ar(self.Rps(b), "sa", sa)
                self.f_evac_sub[s] = (sa, svq)
        for s in range(NSUB):
            p.wait("vector", "sa", self.f_evac_sub[s][0])
            p.wait("vector", "sv", self.f_evac_sub[s][1])
            p.op("vector", lambda e, s=s: e.reduce_sum(
                st(s, ST_S4), st(s, ST_S4P, 16), axis=AX), "sv", 1)
            p.op("vector", lambda e, s=s: e.reduce_sum(
                st(s, ST_SSQ4), st(s, ST_SSQ4P, 16), axis=AX), "sv", 1)
            p.wait("scalar", "sv", p.ctr["sv"])
            fin4 = self.finalize_stats(s, ST_S4, ST_SSQ4, ST_M4, ST_RS4,
                                       ST_NEGMS4, 1e-5)
            p.wait("vector", "sa", fin4)
            t.bw("vector", self.Rx(s))
            t.br("vector", self.Rbb(s))
            p.op("vector", lambda e, s=s: e.tensor_scalar(
                xsub(s), xsub(s), st(s, ST_NEGMS4), None, ALU.add), "sv", 1)
            sv = p.op("vector", lambda e, s=s: e.scalar_tensor_tensor(
                xsub(s), bsub(s), st(s, ST_RS4), xsub(s), ALU.mult, ALU.add),
                "sv", 1)
            t.aw(self.Rx(s), "sv", sv)
            t.ar(self.Rbb(s), "sv", sv)
            p.wait("sync", "sv", sv)
            t.br("sync", self.Rx(s))
            ds = p.dma("sync", lambda e, g=g, s=s: e.dma_start(
                T["y_out"][(g * 512 + s * 128):(g * 512 + (s + 1) * 128), :],
                xsub(s)), f"xl{s}")
            t.ar(self.Rx(s), f"xl{s}", ds)
            self.store_ms[(g, s)] = ds
            if g + 1 < NGRP:
                t.bw("gpsimd", self.Rx(s))
                ds2 = p.dma("gpsimd", lambda e, g=g, s=s: e.dma_start(
                    xsub(s),
                    T["x_in"][((g + 1) * 512 + s * 128):
                              ((g + 1) * 512 + (s + 1) * 128), :]), f"xg{s}")
                t.aw(self.Rx(s), f"xg{s}", ds2)
                self.x0_ms[(g + 1, s)] = ds2
        if g + 1 < NGRP:
            stats_by_sub = {}
            for s in range(NSUB):
                self.a_stats(g + 1, s)
                stats_by_sub[s] = self._a_stats_ms
                if s >= 1:
                    self._a_stats_ms = stats_by_sub[s - 1]
                    self.a_fin(g + 1, s - 1)
            self._a_stats_ms = stats_by_sub[NSUB - 1]
            self.a_fin(g + 1, NSUB - 1)


_NC_CACHE = [None]


def kernel(L0_attn_hidden_states, L0_attn_Wq, L0_attn_bq, L0_attn_Wk,
           L0_attn_bk, L0_attn_Wv, L0_attn_bv, L0_attn_Wo, L0_attn_M,
           L0_attn_z, L0_ffn_lora_A, L0_ffn_lora_B, L0_ffn_W, L0_ffn_bias,
           L0_ffn_lora_weights):
    x = np.ascontiguousarray(
        np.asarray(L0_attn_hidden_states, np.float32).reshape(8192, DM))
    Wq = np.asarray(L0_attn_Wq, np.float32)
    sel = np.concatenate(
        [Wq[4 * j * 128:(4 * j + 1) * 128, :] for j in range(8)], axis=0)
    wqt_h = np.ascontiguousarray(sel.T)
    wot_h = np.ascontiguousarray(np.asarray(L0_attn_Wo, np.float32).T)
    wt_h = np.ascontiguousarray(np.asarray(L0_ffn_W, np.float32).T)
    bq_sel = np.ascontiguousarray(np.concatenate(
        [np.asarray(L0_attn_bq, np.float32)[4 * j * 128:(4 * j + 1) * 128]
         for j in range(8)]))
    A_all = np.asarray(L0_ffn_lora_A, np.float32).reshape(128, DM)
    at_hh = np.ascontiguousarray(A_all.T)
    bt_hh = np.ascontiguousarray(
        np.asarray(L0_ffn_lora_B, np.float32).transpose(0, 2, 1).reshape(128, DM))
    eye = np.eye(P, dtype=np.float32)
    e8 = np.repeat(np.eye(8, dtype=np.float32), 16, axis=1)

    if _NC_CACHE[0] is None:
        _NC_CACHE[0] = build_nc()
    nc = _NC_CACHE[0]

    in_maps = []
    for c in range(NCORE):
        in_maps.append({
            "x": np.ascontiguousarray(x[c * TOK:(c + 1) * TOK, :]),
            "wqt": wqt_h, "wot": wot_h, "wt": wt_h,
            "at_h": at_hh, "bt_h": bt_hh,
            "m_in": np.ascontiguousarray(np.asarray(L0_attn_M, np.float32)),
            "z_in": np.ascontiguousarray(np.asarray(L0_attn_z, np.float32)),
            "bq_in": bq_sel,
            "bias_in": np.ascontiguousarray(np.asarray(L0_ffn_bias, np.float32)),
            "lw_in": np.ascontiguousarray(
                np.asarray(L0_ffn_lora_weights, np.float32)),
            "eye_in": eye, "e8_in": e8,
        })
    res = run_bass_kernel_spmd(nc, in_maps, core_ids=list(range(NCORE)))
    out = np.concatenate([np.asarray(r["y"]) for r in res.results], axis=0)
    return out.reshape(1, 8192, DM)


# version 26
# speedup vs baseline: 1.1162x; 1.1162x over previous
import contextlib
import numpy as np
import concourse.bass as bass
import concourse.mybir as mybir
from concourse.bass_utils import run_bass_kernel_spmd

F32 = mybir.dt.float32
BF16 = mybir.dt.bfloat16
AF = mybir.ActivationFunctionType
ALU = mybir.AluOpType
AX = mybir.AxisListType.X

P = 128
DM = 4096
NCORE = 8
TOK = 1024
NSUB = 4
NGRP = 2
OSL = 256
N_OSL = DM // OSL
ESL = 512
N_ESL = DM // ESL

ENGINES = ("tensor", "scalar", "vector", "gpsimd", "sync")

ST_SUM0, ST_SSQ0, ST_M1, ST_TMP, ST_VAR, ST_SD, ST_RS1, ST_NEGMS1 = range(8)
ST_DEN, ST_DINV = 8, 9
ST_S2, ST_SSQ2, ST_M2, ST_RS2, ST_NEGMS2 = 10, 11, 12, 13, 14
ST_SSQ3, ST_RS3, ST_NEGMS3 = 15, 16, 17
ST_S4, ST_SSQ4, ST_M4, ST_RS4, ST_NEGMS4 = 18, 19, 20, 21, 22
ST_EPS5, ST_EPS6 = 24, 25
ST_DEN8, ST_DINV8 = 96, 104
ST_S2P, ST_SSQ2P = 32, 40
ST_S4P, ST_SSQ4P = 64, 80


class Plan:
    def __init__(self):
        self.ops = {e: [] for e in ENGINES}
        import collections
        self.ctr = collections.defaultdict(int)
        self.pend = {e: {} for e in ENGINES}

    def wait(self, eng, sem, val):
        if val and val > self.pend[eng].get(sem, 0):
            self.pend[eng][sem] = val

    def _flush(self, eng):
        for sem, val in sorted(self.pend[eng].items()):
            self.ops[eng].append(("wait", sem, val))
        self.pend[eng] = {}

    OWN = {"scalar": "sa", "vector": "sv"}

    def op(self, eng, fn, sem=None, inc=0):
        own = self.OWN.get(eng)
        if own is not None and self.ctr[own]:
            self.wait(eng, own, self.ctr[own])
        self._flush(eng)
        self.ops[eng].append(("op", fn, sem, inc))
        if sem is not None:
            self.ctr[sem] += inc
            return self.ctr[sem]
        return None

    def dma(self, eng, fn, sem):
        return self.op(eng, fn, sem, 16)


class Tracker:

    def __init__(self, plan):
        self.p = plan
        self.st = {}

    def _overlaps(self, reg):
        buf, lo, hi = reg
        out = []
        for k in self.st:
            if k[0] == buf and k[1] < hi and lo < k[2]:
                out.append(k)
        return out

    def br(self, eng, reg):
        for k in self._overlaps(reg):
            w = self.st[k]["w"]
            if w:
                self.p.wait(eng, w[0], w[1])

    def ar(self, reg, sem, val):
        s = self.st.setdefault(reg, {"w": None, "r": {}})
        if s["r"].get(sem, 0) < val:
            s["r"][sem] = val

    def bw(self, eng, reg):
        for k in self._overlaps(reg):
            w = self.st[k]["w"]
            if w:
                self.p.wait(eng, w[0], w[1])
            for sem, val in self.st[k]["r"].items():
                self.p.wait(eng, sem, val)

    def aw(self, reg, sem, val):
        buf, lo, hi = reg
        for k in self._overlaps(reg):
            if lo <= k[1] and k[2] <= hi and k != reg:
                del self.st[k]
        self.st[reg] = {"w": (sem, val), "r": {}}


def build_nc():
    nc = bass.Bass("TRN2", target_bir_lowering=False)

    T = {}
    T["x_in"] = nc.dram_tensor("x", [TOK, DM], F32, kind="ExternalInput").ap()
    T["wqt"] = nc.dram_tensor("wqt", [DM, 1024], F32, kind="ExternalInput").ap()
    T["wot"] = nc.dram_tensor("wot", [DM, DM], F32, kind="ExternalInput").ap()
    T["wt"] = nc.dram_tensor("wt", [DM, DM], F32, kind="ExternalInput").ap()
    T["at_h"] = nc.dram_tensor("at_h", [DM, 128], F32, kind="ExternalInput").ap()
    T["bt_h"] = nc.dram_tensor("bt_h", [128, DM], F32, kind="ExternalInput").ap()
    T["m_in"] = nc.dram_tensor("m_in", [8, 128, 128], F32, kind="ExternalInput").ap()
    T["z_in"] = nc.dram_tensor("z_in", [8, 128], F32, kind="ExternalInput").ap()
    T["bq_in"] = nc.dram_tensor("bq_in", [1024], F32, kind="ExternalInput").ap()
    T["bias_in"] = nc.dram_tensor("bias_in", [DM], F32, kind="ExternalInput").ap()
    T["lw_in"] = nc.dram_tensor("lw_in", [8], F32, kind="ExternalInput").ap()
    T["eye_in"] = nc.dram_tensor("eye_in", [P, P], F32, kind="ExternalInput").ap()
    T["e8_in"] = nc.dram_tensor("e8_in", [8, P], F32, kind="ExternalInput").ap()
    T["y_out"] = nc.dram_tensor("y", [TOK, DM], F32, kind="ExternalOutput").ap()
    T["woeff_d"] = nc.dram_tensor("woeff_d", [1024, DM], BF16, kind="Internal").ap()
    T["wqb_d"] = nc.dram_tensor("wqb_d", [DM, 1024], BF16, kind="Internal").ap()
    T["wob_d"] = nc.dram_tensor("wob_d", [DM, DM], BF16, kind="Internal").ap()

    with contextlib.ExitStack() as ctx:
        E = ctx.enter_context
        T["XBUF"] = E(nc.sbuf_tensor("XBUF", [P, 4 * DM], F32))
        T["BIGA"] = E(nc.sbuf_tensor("BIGA", [P, 4 * DM], BF16))
        T["BBUF"] = E(nc.sbuf_tensor("BBUF", [P, 4 * DM], BF16))
        T["WSTR"] = E(nc.sbuf_tensor("WSTR", [P, 4 * DM], BF16))
        T["WSTG"] = E(nc.sbuf_tensor("WSTG", [P, 4 * DM], BF16))
        T["MZ"] = E(nc.sbuf_tensor("MZ", [P, 8 * 130], BF16))
        T["ROWS"] = E(nc.sbuf_tensor("ROWS", [1, 1024 + 128 + 2 * OSL], BF16))
        T["STATS"] = E(nc.sbuf_tensor("STATS", [P, 4 * 128], F32))
        T["EYEB"] = E(nc.sbuf_tensor("EYEB", [P, P], BF16))
        T["E8S"] = E(nc.sbuf_tensor("E8S", [8, P], F32))
        T["ZSTB"] = E(nc.sbuf_tensor("ZSTB", [8, P], BF16))
        T["E8B"] = E(nc.sbuf_tensor("E8B", [8, P], BF16))
        T["WVS"] = E(nc.sbuf_tensor("WVS", [8, 1], F32))
        T["COEF"] = E(nc.sbuf_tensor("COEF", [P, 1], F32))
        T["PS"] = E(nc.psum_tensor("PS", [P, 8 * 512], F32))
        sems = {n: E(nc.semaphore(n)) for n in (
            "sp", "sa", "sv", "dgc", "wq0", "wq1", "wq2", "wq3", "wo0", "wo1", "wt0", "wt1",
            "br0", "br1", "wos0", "wos1", "wos2", "wos3", "wost", "wop0", "wop1", "wqb", "wqb0", "wqb1", "wqb2", "wqb3", "stg", "atb", "sg",
            "xl0", "xl1", "xl2", "xl3", "xg0", "xg1", "xg2", "xg3")}
        block = E(nc.Block())

        plan = Plan()
        trk = Tracker(plan)
        Emit(plan, trk, T).program()

        def replay(name):
            def body(eng):
                for item in plan.ops[name]:
                    if item[0] == "wait":
                        eng.wait_ge(sems[item[1]], item[2])
                    else:
                        _, fn, sem, inc = item
                        ins = fn(eng)
                        if sem is not None:
                            ins.then_inc(sems[sem], inc)
            return body

        block.tensor(replay("tensor"))
        block.scalar(replay("scalar"))
        block.vector(replay("vector"))
        block.gpsimd(replay("gpsimd"))
        block.sync(replay("sync"))
    return nc


class Emit:
    def __init__(self, plan, trk, T):
        self.p = plan
        self.t = trk
        self.T = T
        self.x0_ms = {}
        self.store_ms = {}
        self.ones_ms = 0

    def Rps(self, b):
        return ("PS", b, b + 1)

    def Rx(self, s):
        return ("XB", s, s + 1)

    def Rbb(self, s):
        return ("BB", s, s + 1)

    def Rga(self, s):
        return ("GA", s, s + 1)

    def Rwq(self, par):
        return ("WSTR", par * 1024, (par + 1) * 1024)

    def Rretstg(self):
        return ("WSTR", 13312, 14336)

    def Rrett(self, s):
        return ("WSTR", 8192 + s * 1024, 8192 + (s + 1) * 1024)

    def Rqsel(self):
        return ("WSTR", 12288, 13312)

    def Rwo(self, par):
        return ("WSTR", par * DM, (par + 1) * DM)

    def Rwt(self, par):
        return ("WSTR", par * 8192, (par + 1) * 8192)

    def Rwoev(self):
        return ("WSTR", 0, DM)

    def Rat(self):
        return ("WSTG", 0, DM)

    def Rbt(self):
        return ("WSTG", DM, 2 * DM)

    def Rbtstg(self):
        return ("WSTG", 2 * DM, 3 * DM)

    def Rwostg(self, par):
        return ("WSTG", (2 + par) * DM, (3 + par) * DM)

    def Rdump(self):
        return ("WSTG", 3 * DM + 512, 3 * DM + 1024)

    def Rhtt(self, s):
        return ("WSTG", 3 * DM + 1024 + s * 128, 3 * DM + 1024 + (s + 1) * 128)

    def Rhb(self):
        return ("WSTG", 3 * DM + 1536, 3 * DM + 1664)

    def Rdump2(self):
        return ("WSTG", 3 * DM + 2048, 3 * DM + 2048 + OSL)

    def Rbias(self, par):
        return ("ROWS", 1152 + par * OSL, 1152 + (par + 1) * OSL)

    def st(self, s, slot, n=1):
        return self.T["STATS"][:, s * 128 + slot: s * 128 + slot + n]

    def bank(self, b, off=0, width=512):
        return self.T["PS"][:, b * 512 + off: b * 512 + off + width]

    def xsub(self, s):
        return self.T["XBUF"][:, s * DM:(s + 1) * DM]

    def bsub(self, s, a=0, b=DM):
        return self.T["BBUF"][:, s * DM + a: s * DM + b]

    def gsub(self, s, a=0, b=DM):
        return self.T["BIGA"][:, s * DM + a: s * DM + b]

    def wstr(self, a, b):
        return self.T["WSTR"][:, a:b]

    def wstg(self, a, b):
        return self.T["WSTG"][:, a:b]

    def ones(self):
        return self.T["ROWS"][0:1, 1024:1152][0:1, 0:128]

    def bqr(self, h):
        return self.T["ROWS"][0:1, h * 512:(h + 1) * 512]

    def biasr(self, par):
        return self.T["ROWS"][0:1, 1152 + par * OSL:1152 + (par + 1) * OSL]

    def evac(self, k, dst_ap_fn, bank_b, bank_off, width, dst_reg, pe_ms):
        eng = "scalar" if k % 2 == 0 else "vector"
        sem = "sa" if k % 2 == 0 else "sv"
        p, t = self.p, self.t
        p.wait(eng, "sp", pe_ms)
        t.br(eng, self.Rps(bank_b))
        t.bw(eng, dst_reg)
        if eng == "scalar":
            ms = p.op(eng, lambda e: e.activation(
                dst_ap_fn(), self.bank(bank_b, bank_off, width), AF.Copy), sem, 1)
        else:
            ms = p.op(eng, lambda e: e.tensor_copy(
                dst_ap_fn(), self.bank(bank_b, bank_off, width)), sem, 1)
        t.ar(self.Rps(bank_b), sem, ms)
        t.aw(dst_reg, sem, ms)
        return ms

    def trans_block(self, grp, srcs, dst_fn, banks2, src_reg, dst_reg,
                    eng_force=None):
        p, t = self.p, self.t
        b = banks2[grp % 2]
        t.bw("tensor", self.Rps(b))
        t.br("tensor", src_reg)
        for k in range(4):
            pe = p.op("tensor", lambda e, k=k, b=b, f=srcs[k]: e.matmul(
                self.bank(b, k * 128, 128), f(), self.T["EYEB"][:],
                start=True, stop=True), "sp", 1 if k == 3 else 0)
        t.aw(self.Rps(b), "sp", pe)
        t.ar(src_reg, "sp", pe)
        eng = eng_force or ("scalar" if grp % 2 == 0 else "vector")
        sem = "sa" if eng == "scalar" else "sv"
        p.wait(eng, "sp", pe)
        t.br(eng, self.Rps(b))
        t.bw(eng, dst_reg)
        if eng == "scalar":
            ms = p.op(eng, lambda e, b=b: e.activation(
                dst_fn(), self.bank(b, 0, 512), AF.Copy), sem, 1)
        else:
            ms = p.op(eng, lambda e, b=b: e.tensor_copy(
                dst_fn(), self.bank(b, 0, 512)), sem, 1)
        t.ar(self.Rps(b), sem, ms)
        t.aw(dst_reg, sem, ms)
        return ms

    def finalize_stats(self, s, slot_sum, slot_ssq, slot_m, slot_rs,
                       slot_negms, eps, reuse_m=False):
        p = self.p
        st = self.st
        if not reuse_m:
            p.op("scalar", lambda e: e.activation(
                st(s, slot_m), st(s, slot_sum), AF.Copy, scale=1.0 / DM),
                "sa", 1)
        p.op("scalar", lambda e: e.activation(
            st(s, ST_TMP), st(s, slot_m), AF.Square), "sa", 1)
        p.op("scalar", lambda e: e.activation(
            st(s, ST_SD), st(s, ST_TMP), AF.Copy, scale=-1.0), "sa", 1)
        p.op("scalar", lambda e: e.activation(
            st(s, ST_VAR), st(s, slot_ssq), AF.Identity, scale=1.0 / DM,
            bias=st(s, ST_SD)), "sa", 1)
        p.op("scalar", lambda e: e.activation(
            st(s, ST_TMP), st(s, ST_VAR), AF.Ln, bias=st(s, ST_EPS5)), "sa", 1)
        p.op("scalar", lambda e: e.activation(
            st(s, slot_rs), st(s, ST_TMP), AF.Exp, scale=-0.5), "sa", 1)
        p.op("scalar", lambda e: e.activation(
            st(s, ST_DEN), st(s, slot_m), AF.Copy, scale=-1.0), "sa", 1)
        sa = p.op("scalar", lambda e: e.activation(
            st(s, slot_negms), st(s, ST_DEN), AF.Identity,
            scale=st(s, slot_rs)), "sa", 1)
        return sa

    def program(self):
        p, t, T = self.p, self.t, self.T

        p.dma("gpsimd", lambda e: e.dma_start(T["EYEB"][:], T["eye_in"]), "dgc")
        p.dma("gpsimd", lambda e: e.dma_start(T["E8S"][:], T["e8_in"]), "dgc")
        p.dma("gpsimd", lambda e: e.dma_start(
            T["WVS"][:], T["lw_in"].rearrange("(p a) -> p a", a=1)), "dgc")
        for j in range(8):
            p.dma("gpsimd", lambda e, j=j: e.dma_start(
                T["MZ"][:, j * 130:j * 130 + 128],
                bass.AP(T["m_in"].tensor, j * 128 * 128, [[128, 128], [1, 128]])),
                "stg")
        p.dma("gpsimd", lambda e: e.dma_start(T["ZSTB"][:], T["z_in"]), "stg")
        p.dma("gpsimd", lambda e: e.dma_start(T["E8B"][:], T["e8_in"]), "stg")
        p.dma("gpsimd", lambda e: e.dma_start(
            T["ROWS"][0:1, 0:1024],
            bass.AP(T["bq_in"].tensor, 0, [[1, 1], [1, 1024]])), "stg")
        self.ones_ms = p.op("vector", lambda e: e.memset(self.ones(), 1.0), "sv", 1)
        for s in range(NSUB):
            p.op("vector", lambda e, s=s: e.memset(self.st(s, ST_EPS5), 1e-5),
                 "sv", 1)
            self.eps_ms = p.op("vector", lambda e, s=s: e.memset(
                self.st(s, ST_EPS6), 1e-6), "sv", 1)

        t.bw("gpsimd", self.Rat())
        p.dma("gpsimd", lambda e: e.dma_start(
            self.wstg(0, DM).rearrange("p (c n) -> p c n", n=128),
            bass.AP(T["at_h"].tensor, 0, [[128, 128], [128 * 128, 32], [1, 128]])),
            "atb")
        t.bw("gpsimd", self.Rbtstg())
        dg_ab = p.dma("gpsimd", lambda e: e.dma_start(
            self.wstg(2 * DM, 3 * DM), T["bt_h"]), "atb")
        self.dgc_all = p.ctr["dgc"]
        self.stg_all = p.ctr["stg"]
        self.atb_all = p.ctr["atb"]
        t.aw(self.Rat(), "atb", self.atb_all)
        t.aw(self.Rbtstg(), "atb", self.atb_all)

        self.wq_cast_ms = {}
        for c in range(4):
            self.wq_cast_ms[c] = p.dma("gpsimd", lambda e, c=c: e.dma_start(
                bass.AP(T["wqb_d"].tensor, c * 1024 * 1024, [[1024, 1024], [1, 1024]]),
                bass.AP(T["wqt"].tensor, c * 1024 * 1024, [[1024, 1024], [1, 1024]])),
                f"wqb{c}")



        self.main_loads()
        self.prep_queue = list(range(8))
        self.phase_A(0)
        self.prep_some(8)
        self.wo_prep_done = (p.ctr["wop0"], p.ctr["wop1"])
        self.phase_B(0)
        self.phase_F(0)
        for g in range(1, NGRP):
            self.phase_A(g)
            self.phase_B(g)
            self.phase_F(g)
        for s in range(NSUB):
            p.wait("sync", f"xl{s}", p.ctr[f"xl{s}"])
        p._flush("sync")

    def coef_bt(self):
        p, t, T = self.p, self.t, self.T

        p.wait("tensor", "dgc", self.dgc_all)
        t.bw("tensor", self.Rps(0))
        pe = p.op("tensor", lambda e: e.matmul(
            self.bank(0, 0, 1), T["E8S"][:], T["WVS"][:], start=True, stop=True),
            "sp", 1)
        t.aw(self.Rps(0), "sp", pe)
        p.wait("scalar", "sp", pe)
        sa = p.op("scalar", lambda e: e.activation(
            T["COEF"][:], self.bank(0, 0, 1), AF.Copy, scale=2.0), "sa", 1)
        t.ar(self.Rps(0), "sa", sa)

        p.wait("tensor", "stg", p.ctr["stg"])
        t.bw("tensor", self.Rps(1))
        pe = p.op("tensor", lambda e: e.matmul(
            self.bank(1, 0, 128), T["ZSTB"][:], T["E8B"][:],
            start=True, stop=True), "sp", 1)
        t.aw(self.Rps(1), "sp", pe)
        p.wait("scalar", "sp", pe)
        sa = p.op("scalar", lambda e: e.activation(
            T["MZ"][:].rearrange("p (j c) -> p j c", c=130)[:, :, 128:129],
            T["PS"][:, 512:640].rearrange("p (j c) -> p j c", c=16)[:, :, 0:1],
            AF.Copy), "sa", 1)
        t.ar(self.Rps(1), "sa", sa)
        self.zcols_ms = sa

        p.wait("vector", "sa", sa)
        p.wait("vector", "atb", self.atb_all)
        t.bw("vector", self.Rbt())
        t.br("vector", self.Rbtstg())
        sv = p.op("vector", lambda e: e.tensor_scalar_mul(
            self.wstg(DM, 2 * DM), self.wstg(2 * DM, 3 * DM), T["COEF"][:, 0:1]),
            "sv", 1)
        t.aw(self.Rbt(), "sv", sv)
        t.ar(self.Rbtstg(), "sv", sv)

    def prep_some(self, n):
        for _ in range(n):
            if self.prep_queue:
                self.wo_prep_j(self.prep_queue.pop(0))

    def wo_prep_j(self, j):
        p, t, T = self.p, self.t, self.T
        for eh in range(2):
            for g in range(4):
                slot = (2 * eh + g) % 4
                lo = 2 * DM + slot * 2048
                Rst = ("WSTG", lo, lo + 2048)
                t.bw("gpsimd", Rst)
                dg = p.dma("gpsimd", lambda e, j=j, g=g, eh=eh, lo=lo: e.dma_start(
                    self.wstg(lo, lo + 2048),
                    bass.AP(T["wot"].tensor,
                            (4 * j + g) * 128 * DM + eh * 2048,
                            [[DM, 128], [1, 2048]])), f"wos{slot}")
                t.aw(Rst, f"wos{slot}", dg)
                p.wait("tensor", f"wos{slot}", dg)
                t.br("tensor", Rst)
                if g == 0:
                    for o in range(4):
                        t.bw("tensor", self.Rps(4 + o))
                for o in range(4):
                    pe = p.op("tensor", lambda e, g=g, o=o, lo=lo: e.matmul(
                        self.bank(4 + o), T["EYEB"][:],
                        self.wstg(lo + o * 512, lo + (o + 1) * 512),
                        start=(g == 0), stop=(g == 3)),
                        "sp", 1 if o == 3 else 0)
                t.ar(Rst, "sp", pe)
            for o in range(4):
                t.aw(self.Rps(4 + o), "sp", pe)
            Rev = ("WSTR", eh * 2048, (eh + 1) * 2048)
            t.bw("scalar", Rev)
            p.wait("scalar", "sp", pe)
            for o in range(4):
                t.br("scalar", self.Rps(4 + o))
                sa = p.op("scalar", lambda e, o=o, eh=eh: e.activation(
                    self.wstr(eh * 2048 + o * 512, eh * 2048 + (o + 1) * 512),
                    self.bank(4 + o), AF.Copy), "sa", 1 if o == 3 else 0)
            for o in range(4):
                t.ar(self.Rps(4 + o), "sa", sa)
            t.aw(Rev, "sa", sa)
            t.br("sync", Rev)
            ds = p.dma("sync", lambda e, j=j, eh=eh: e.dma_start(
                bass.AP(T["woeff_d"].tensor, j * 128 * DM + eh * 2048,
                        [[DM, 128], [1, 2048]]),
                self.wstr(eh * 2048, (eh + 1) * 2048)), f"wop{eh}")
            t.ar(Rev, f"wop{eh}", ds)

    def main_loads(self):
        p, t, T = self.p, self.t, self.T
        for s in range(NSUB):
            t.bw("sync", self.Rx(s))
            ds = p.dma("sync", lambda e, s=s: e.dma_start(
                self.xsub(s), T["x_in"][s * 128:(s + 1) * 128, :]), f"xl{s}")
            t.aw(self.Rx(s), f"xl{s}", ds)
            self.x0_ms[(0, s)] = ds

    def a_stats(self, g, s):
        p, t, T = self.p, self.t, self.T
        st = self.st
        xsub, bsub = self.xsub, self.bsub
        xsem = f"xl{s}" if g == 0 else f"xg{s}"
        p.wait("vector", xsem, self.x0_ms[(g, s)])
        p.wait("scalar", xsem, self.x0_ms[(g, s)])
        if g == 0:
            t.br("vector", self.Rx(s))
            sv = p.op("vector", lambda e, s=s: e.reduce_sum(
                st(s, ST_SUM0), xsub(s), axis=AX), "sv", 1)
            t.ar(self.Rx(s), "sv", sv)
            self._a_stats_sv = sv
        else:
            t.bw("scalar", self.Rbb(s))
            t.br("scalar", self.Rx(s))
            sa = p.op("scalar", lambda e, s=s: e.activation(
                bsub(s), xsub(s), AF.Copy, accum_out=st(s, ST_SUM0)),
                "sa", 1)
            t.aw(self.Rbb(s), "sa", sa)
            t.ar(self.Rx(s), "sa", sa)
        t.bw("scalar", self.Rbb(s))
        t.br("scalar", self.Rx(s))
        sa = p.op("scalar", lambda e, s=s: e.activation(
            bsub(s), xsub(s), AF.Square, accum_out=st(s, ST_SSQ0)),
            "sa", 1)
        t.aw(self.Rbb(s), "sa", sa)
        t.ar(self.Rx(s), "sa", sa)
        self._a_stats_ms = sa

    def a_fin(self, g, s):
        p, t, T = self.p, self.t, self.T
        st = self.st
        xsub, bsub, gsub = self.xsub, self.bsub, self.gsub
        p.wait("scalar", "sa", self._a_stats_ms)
        p.wait("scalar", "sv", self._a_stats_sv)
        self.finalize_stats(s, ST_SUM0, ST_SSQ0, ST_M1, ST_RS1,
                            ST_NEGMS1, 1e-5)
        t.bw("scalar", self.Rbb(s))
        t.br("scalar", self.Rx(s))
        sa = p.op("scalar", lambda e, s=s: e.activation(
            bsub(s), xsub(s), AF.Identity, bias=st(s, ST_NEGMS1),
            scale=st(s, ST_RS1)), "sa", 1)
        t.aw(self.Rbb(s), "sa", sa)
        t.ar(self.Rx(s), "sa", sa)
        p.wait("tensor", "sa", sa)
        for G in range(8):
            self.trans_block(
                G,
                [(lambda s=s, fc=G * 4 + k: bsub(s, fc * 128, (fc + 1) * 128))
                 for k in range(4)],
                (lambda s=s, G=G: gsub(s, G * 512, (G + 1) * 512)),
                (0, 1), self.Rbb(s), self.Rga(s))

    def phase_A(self, g):
        p, t, T = self.p, self.t, self.T
        st, bank = self.st, self.bank
        xsub, bsub, gsub = self.xsub, self.bsub, self.gsub

        if g == 0:
            p.wait("tensor", "dgc", self.dgc_all)
            for s in range(NSUB):
                self.a_stats(g, s)
                self.a_fin(g, s)
            self.coef_bt()

        qdone = {}
        for fc in range(32):
            par = fc % 4
            t.bw("sync", self.Rwq(par))
            if fc % 8 == 0:
                p.wait("sync", f"wqb{fc // 8}", self.wq_cast_ms[fc // 8])
            dg = p.dma("sync", lambda e, fc=fc, par=par: e.dma_start(
                self.wstr(par * 1024, (par + 1) * 1024),
                bass.AP(T["wqb_d"].tensor, fc * 128 * 1024, [[1024, 128], [1, 1024]])),
                f"wq{par}")
            t.aw(self.Rwq(par), f"wq{par}", dg)
            p.wait("tensor", f"wq{par}", dg)
            t.br("tensor", self.Rwq(par))
            for s in range(NSUB):
                t.br("tensor", self.Rga(s))
                for h in range(2):
                    b = 2 * s + h
                    if fc == 0:
                        t.bw("tensor", self.Rps(b))
                    pe = p.op("tensor",
                              lambda e, s=s, h=h, fc=fc, b=b, par=par: e.matmul(
                                  bank(b), gsub(s, fc * 128, (fc + 1) * 128),
                                  self.wstr(par * 1024 + h * 512,
                                            par * 1024 + (h + 1) * 512),
                                  start=(fc == 0), stop=False), "sp", 1)
            t.ar(self.Rwq(par), "sp", pe)
            for s in range(NSUB):
                t.ar(self.Rga(s), "sp", pe)
        p.wait("tensor", "sv", self.ones_ms)
        p.wait("tensor", "stg", self.stg_all)
        p.wait("tensor", "sa", self.zcols_ms)
        for s in range(NSUB):
            for h in range(2):
                b = 2 * s + h
                pe = p.op("tensor", lambda e, h=h, b=b: e.matmul(
                    bank(b), self.ones(), self.bqr(h), start=False, stop=True),
                    "sp", 1)
                t.aw(self.Rps(b), "sp", pe)
                qdone[(s, h)] = pe

        sig_ms = {}
        for s in range(NSUB):
            for h in range(2):
                b = 2 * s + h
                pe = qdone[(s, h)]
                p.wait("scalar", "sp", pe)
                t.br("scalar", self.Rps(b))
                t.bw("scalar", self.Rbb(s))
                sa = p.op("scalar", lambda e, s=s, h=h, b=b: e.activation(
                    bsub(s, 1024 + h * 512, 1024 + (h + 1) * 512), bank(b),
                    AF.Relu), "sa", 1)
                t.ar(self.Rps(b), "sa", sa)
                t.aw(self.Rbb(s), "sa", sa)
                p.wait("vector", "sa", sa)
                p.wait("vector", "sp", pe)
                t.br("vector", self.Rps(b))
                t.bw("vector", self.Rbb(s))
                sv = p.op("vector", lambda e, s=s, h=h, b=b: e.tensor_tensor(
                    bsub(s, 2048 + h * 512, 2048 + (h + 1) * 512), bank(b),
                    bsub(s, 1024 + h * 512, 1024 + (h + 1) * 512), ALU.subtract),
                    "sv", 1)
                t.ar(self.Rps(b), "sv", sv)
                t.aw(self.Rbb(s), "sv", sv)
                p.wait("scalar", "sv", sv)
                sa = p.op("scalar", lambda e, s=s, h=h: e.activation(
                    bsub(s, 3072 + h * 512, 3072 + (h + 1) * 512),
                    bsub(s, 2048 + h * 512, 2048 + (h + 1) * 512), AF.Exp),
                    "sa", 1)
                p.wait("vector", "sa", sa)
                sv = p.op("vector", lambda e, s=s, h=h: e.tensor_tensor(
                    bsub(s, h * 512, (h + 1) * 512),
                    bsub(s, 1024 + h * 512, 1024 + (h + 1) * 512),
                    bsub(s, 3072 + h * 512, 3072 + (h + 1) * 512), ALU.add),
                    "sv", 1)
                t.aw(self.Rbb(s), "sv", sv)
            sig_ms[s] = sv
        if g == 0:
            self.prep_some(1)

        for s in range(NSUB):
            p.wait("tensor", "sv", sig_ms[s])
            for G in range(2):
                self.trans_block(
                    G,
                    [(lambda s=s, j=G * 4 + k: bsub(s, j * 128, (j + 1) * 128))
                     for k in range(4)],
                    (lambda G=G: self.wstr(12288 + G * 512, 12288 + (G + 1) * 512)),
                    (0, 1), self.Rbb(s), self.Rqsel())
            t.br("tensor", self.Rqsel())
            for r in range(2):
                for k in range(4):
                    j = r * 4 + k
                    t.bw("tensor", self.Rps(k))
                    pe = p.op("tensor", lambda e, j=j, k=k: e.matmul(
                        bank(k, 0, 129),
                        self.wstr(12288 + j * 128, 12288 + (j + 1) * 128),
                        T["MZ"][:, j * 130:j * 130 + 129], start=True, stop=True),
                        "sp", 1)
                    t.aw(self.Rps(k), "sp", pe)
                t.ar(self.Rqsel(), "sp", pe)
                p.wait("scalar", "sp", pe)
                for k in range(4):
                    j = r * 4 + k
                    sa = p.op("scalar", lambda e, s=s, j=j, k=k: e.activation(
                        st(s, ST_DEN8 + j), bank(k, 128, 1), AF.Identity,
                        bias=st(s, ST_EPS6)), "sa", 1)
                p.wait("vector", "sa", sa)
                sv = p.op("vector", lambda e, s=s, r=r: e.reciprocal(
                    st(s, ST_DINV8 + r * 4, 4), st(s, ST_DEN8 + r * 4, 4)),
                    "sv", 1)
                p.wait("scalar", "sv", sv)
                t.bw("scalar", self.Rretstg())
                t.bw("vector", self.Rretstg())
                for k in range(4):
                    j = r * 4 + k
                    if k % 2 == 0:
                        sa = p.op("scalar", lambda e, s=s, j=j, k=k: e.activation(
                            self.wstr(13312 + j * 128, 13312 + (j + 1) * 128),
                            bank(k, 0, 128), AF.Copy, scale=st(s, ST_DINV8 + j)),
                            "sa", 1)
                        t.ar(self.Rps(k), "sa", sa)
                    else:
                        svr = p.op("vector", lambda e, s=s, j=j, k=k: e.tensor_scalar_mul(
                            self.wstr(13312 + j * 128, 13312 + (j + 1) * 128),
                            bank(k, 0, 128), st(s, ST_DINV8 + j)), "sv", 1)
                        t.ar(self.Rps(k), "sv", svr)
                t.aw(self.Rretstg(), "sa", sa)
                t.ar(self.Rretstg(), "sv", svr)
                p.wait("tensor", "sv", svr)
                p.wait("tensor", "sa", sa)
                self.trans_block(
                    r,
                    [(lambda j=r * 4 + k: self.wstr(13312 + j * 128,
                                                    13312 + (j + 1) * 128))
                     for k in range(4)],
                    (lambda s=s, r=r: self.wstr(8192 + s * 1024 + r * 512,
                                                8192 + s * 1024 + (r + 1) * 512)),
                    (2, 3), self.Rretstg(), self.Rrett(s))
            if g == 0:
                self.prep_some(1 if s < 3 else 2)

    def phase_B(self, g):
        p, t, T = self.p, self.t, self.T
        st, bank = self.st, self.bank
        xsub, bsub, gsub = self.xsub, self.bsub, self.gsub
        self.b_evac_sub = {}
        self.htt_ms = {}

        for i in range(N_ESL):
            par = i % 2
            t.bw("sync", self.Rwo(par))
            if g == 0 and i < 2:
                p.wait("sync", "wop0", self.wo_prep_done[0])
                p.wait("sync", "wop1", self.wo_prep_done[1])
            ds = p.dma("sync", lambda e, i=i, par=par: e.dma_start(
                self.wstr(par * DM, (par + 1) * DM).rearrange(
                    "p (j x) -> p j x", x=512),
                bass.AP(T["woeff_d"].tensor, i * 512,
                        [[DM, 128], [128 * DM, 8], [1, 512]])), f"wo{par}")
            t.aw(self.Rwo(par), f"wo{par}", ds)
            p.wait("tensor", f"wo{par}", ds)
            t.br("tensor", self.Rwo(par))
            self.b_slice(g, i, par, (0, 1, 2, 3))

    def b_slice(self, g, i, par, subs):
        p, t, T = self.p, self.t, self.T
        st, bank = self.st, self.bank
        bsub = self.bsub
        if True:
            for s in subs:
                b = par * 4 + s
                t.bw("tensor", self.Rps(b))
                t.br("tensor", self.Rrett(s))
                for j in range(8):
                    pe = p.op("tensor",
                              lambda e, s=s, j=j, b=b, par=par: e.matmul(
                                  bank(b),
                                  self.wstr(8192 + s * 1024 + j * 128,
                                            8192 + s * 1024 + (j + 1) * 128),
                                  self.wstr(par * DM + j * 512,
                                            par * DM + (j + 1) * 512),
                                  start=(j == 0), stop=(j == 7)),
                              "sp", 1 if j == 7 else 0)
                t.aw(self.Rps(b), "sp", pe)
                t.ar(self.Rrett(s), "sp", pe)
                t.ar(self.Rwo(par), "sp", pe)
                p.wait("scalar", "sp", pe)
                t.br("scalar", self.Rps(b))
                t.bw("scalar", self.Rbb(s))
                sa = p.op("scalar", lambda e, s=s, i=i, b=b: e.activation(
                    bsub(s, i * 512, (i + 1) * 512), bank(b), AF.Copy,
                    accum_out=st(s, ST_S2P + i)), "sa", 1)
                t.aw(self.Rbb(s), "sa", sa)
                t.bw("scalar", self.Rdump())
                p.wait("vector", "sa", sa)
                t.bw("vector", self.Rdump())
                svq = p.op("vector", lambda e, s=s, i=i: e.scalar_tensor_tensor(
                    self.wstg(3 * DM + 512, 3 * DM + 1024),
                    bsub(s, i * 512, (i + 1) * 512), 1.0,
                    bsub(s, i * 512, (i + 1) * 512), ALU.mult, ALU.mult,
                    accum_out=st(s, ST_SSQ2P + i)), "sv", 1)
                t.aw(self.Rdump(), "sv", svq)
                t.ar(self.Rps(b), "sa", sa)
                self.b_evac_sub[s] = (sa, svq)

    def fp_reduce(self, s):
        p, st = self.p, self.st
        p.wait("vector", "sa", self.b_evac_sub[s][0])
        p.wait("vector", "sv", self.b_evac_sub[s][1])
        p.op("vector", lambda e, s=s: e.reduce_sum(
            st(s, ST_S2), st(s, ST_S2P, 8), axis=AX), "sv", 1)
        sv = p.op("vector", lambda e, s=s: e.reduce_sum(
            st(s, ST_SSQ2), st(s, ST_SSQ2P, 8), axis=AX), "sv", 1)
        self._fp_red[s] = sv

    def fp_fin2(self, s):
        p = self.p
        p.wait("scalar", "sv", self._fp_red[s])
        self._fp_fin2[s] = self.finalize_stats(
            s, ST_S2, ST_SSQ2, ST_M2, ST_RS2, ST_NEGMS2, 1e-5)

    def fp_resid(self, s):
        p, t = self.p, self.t
        st, xsub, bsub, gsub = self.st, self.xsub, self.bsub, self.gsub
        p.wait("vector", "sa", self._fp_fin2[s])
        t.bw("vector", self.Rx(s))
        t.br("vector", self.Rbb(s))
        p.op("vector", lambda e, s=s: e.tensor_scalar(
            xsub(s), xsub(s), st(s, ST_NEGMS2), None, ALU.add), "sv", 1)
        sv = p.op("vector", lambda e, s=s: e.scalar_tensor_tensor(
            xsub(s), bsub(s), st(s, ST_RS2), xsub(s), ALU.mult, ALU.add),
            "sv", 1)
        t.aw(self.Rx(s), "sv", sv)
        t.ar(self.Rbb(s), "sv", sv)
        t.bw("vector", self.Rga(s))
        sv3 = p.op("vector", lambda e, s=s: e.scalar_tensor_tensor(
            gsub(s), xsub(s), 1.0, xsub(s), ALU.mult, ALU.mult,
            accum_out=st(s, ST_SSQ3)), "sv", 1)
        t.aw(self.Rga(s), "sv", sv3)
        t.ar(self.Rx(s), "sv", sv3)
        self._fp_ssq3[s] = sv3

    def f_pre(self, g, s):
        p, t, T = self.p, self.t, self.T
        st, bank = self.st, self.bank
        xsub, bsub, gsub = self.xsub, self.bsub, self.gsub
        if True:
            p.wait("scalar", "sv", self._fp_ssq3[s])
            self.finalize_stats(s, ST_SUM0, ST_SSQ3, ST_M1, ST_RS3,
                                 ST_NEGMS3, 1e-5, reuse_m=True)
            t.bw("scalar", self.Rbb(s))
            t.br("scalar", self.Rx(s))
            sa = p.op("scalar", lambda e, s=s: e.activation(
                bsub(s), xsub(s), AF.Identity, bias=st(s, ST_NEGMS3),
                scale=st(s, ST_RS3)), "sa", 1)
            t.aw(self.Rbb(s), "sa", sa)
            t.ar(self.Rx(s), "sa", sa)
            p.wait("tensor", "sa", sa)
            for G in range(8):
                self.trans_block(
                    G,
                    [(lambda s=s, fc=G * 4 + k: bsub(s, fc * 128, (fc + 1) * 128))
                     for k in range(4)],
                    (lambda s=s, G=G: gsub(s, G * 512, (G + 1) * 512)),
                    (0, 1), self.Rbb(s), self.Rga(s), eng_force="scalar")
            b = 2 + (s % 2)
            t.bw("tensor", self.Rps(b))
            t.br("tensor", self.Rga(s))
            t.br("tensor", self.Rat())
            for fc in range(32):
                pe = p.op("tensor", lambda e, s=s, fc=fc, b=b: e.matmul(
                    bank(b, 0, 128), gsub(s, fc * 128, (fc + 1) * 128),
                    self.wstg(fc * 128, (fc + 1) * 128),
                    start=(fc == 0), stop=(fc == 31)), "sp", 1 if fc == 31 else 0)
            t.aw(self.Rps(b), "sp", pe)
            t.ar(self.Rga(s), "sp", pe)
            t.ar(self.Rat(), "sp", pe)
            p.wait("scalar", "sp", pe)
            t.br("scalar", self.Rps(b))
            t.bw("scalar", self.Rhb())
            sa = p.op("scalar", lambda e, b=b: e.activation(
                self.wstg(3 * DM + 1536, 3 * DM + 1664), bank(b, 0, 128),
                AF.Copy), "sa", 1)
            t.ar(self.Rps(b), "sa", sa)
            t.aw(self.Rhb(), "sa", sa)
            p.wait("tensor", "sa", sa)
            t.bw("tensor", self.Rps(b))
            t.br("tensor", self.Rhb())
            pe = p.op("tensor", lambda e, b=b: e.matmul(
                bank(b, 384, 128), self.wstg(3 * DM + 1536, 3 * DM + 1664),
                T["EYEB"][:], start=True, stop=True), "sp", 1)
            t.aw(self.Rps(b), "sp", pe)
            t.ar(self.Rhb(), "sp", pe)
            p.wait("scalar", "sp", pe)
            t.br("scalar", self.Rps(b))
            t.bw("scalar", self.Rhtt(s))
            sa = p.op("scalar", lambda e, s=s, b=b: e.activation(
                self.wstg(3 * DM + 1024 + s * 128, 3 * DM + 1024 + (s + 1) * 128),
                bank(b, 384, 128), AF.Copy), "sa", 1)
            t.ar(self.Rps(b), "sa", sa)
            t.aw(self.Rhtt(s), "sa", sa)
            self.htt_ms[s] = sa

    def phase_F(self, g):
        p, t, T = self.p, self.t, self.T
        st, bank = self.st, self.bank
        xsub, bsub, gsub = self.xsub, self.bsub, self.gsub
        self._fp_red, self._fp_fin2, self._fp_ssq3 = {}, {}, {}
        for s in range(NSUB):
            self.fp_reduce(s)
        for s in range(NSUB):
            self.fp_fin2(s)
        for s in range(NSUB):
            self.fp_resid(s)
        for s in range(NSUB):
            self.f_pre(g, s)
        htt_ms = self.htt_ms

        self.f_evac_sub = {}
        for i in range(N_OSL):
            par = i % 2
            t.bw("gpsimd", self.Rbias(par))
            dgb = p.dma("gpsimd", lambda e, i=i, par=par: e.dma_start(
                self.biasr(par),
                bass.AP(T["bias_in"].tensor, i * OSL, [[1, 1], [1, OSL]])),
                f"br{par}")
            t.aw(self.Rbias(par), f"br{par}", dgb)
            t.bw("gpsimd", self.Rwt(par))
            for q in range(4):
                dg = p.dma("gpsimd", lambda e, i=i, q=q, par=par: e.dma_start(
                    self.wstr(par * 8192 + q * 2048,
                              par * 8192 + (q + 1) * 2048).rearrange(
                        "p (c o) -> p c o", o=OSL),
                    bass.AP(T["wt"].tensor, (8 * q * 128) * DM + i * OSL,
                            [[DM, 128], [128 * DM, 8], [1, OSL]])), f"wt{par}")
            t.aw(self.Rwt(par), f"wt{par}", dg)
            p.wait("tensor", f"wt{par}", dg)
            p.wait("tensor", f"br{par}", dgb)
            t.br("tensor", self.Rwt(par))
            t.br("tensor", self.Rbias(par))
            t.br("tensor", self.Rbt())
            chain_pe = {}
            for s in range(NSUB):
                b = par * 2 + s // 2
                off = (s % 2) * 256
                if s % 2 == 0:
                    t.bw("tensor", self.Rps(b))
                t.br("tensor", self.Rga(s))
                for fc in range(32):
                    p.op("tensor",
                         lambda e, s=s, fc=fc, b=b, off=off, par=par: e.matmul(
                             bank(b, off, OSL), gsub(s, fc * 128, (fc + 1) * 128),
                             self.wstr(par * 8192 + fc * OSL,
                                       par * 8192 + (fc + 1) * OSL),
                             start=(fc == 0), stop=False), "sp", 0)
                p.wait("tensor", "sa", htt_ms[s])
                t.br("tensor", self.Rhtt(s))
                p.op("tensor", lambda e, s=s, b=b, off=off, i=i: e.matmul(
                    bank(b, off, OSL),
                    self.wstg(3 * DM + 1024 + s * 128, 3 * DM + 1024 + (s + 1) * 128),
                    self.wstg(DM + i * OSL, DM + (i + 1) * OSL),
                    start=False, stop=False), "sp", 0)
                pe = p.op("tensor", lambda e, b=b, off=off, par=par: e.matmul(
                    bank(b, off, OSL), self.ones(), self.biasr(par),
                    start=False, stop=True), "sp", 1)
                chain_pe[s] = pe
                t.aw(self.Rps(b), "sp", pe)
                t.ar(self.Rga(s), "sp", pe)
                t.ar(self.Rwt(par), "sp", pe)
                t.ar(self.Rbias(par), "sp", pe)
                t.ar(self.Rbt(), "sp", pe)
                t.ar(self.Rhtt(s), "sp", pe)
            for s in range(NSUB):
                b = par * 2 + s // 2
                off = (s % 2) * 256
                p.wait("scalar", "sp", max(chain_pe[s], chain_pe[s ^ 1]))
                t.br("scalar", self.Rps(b))
                t.bw("scalar", self.Rbb(s))
                sa = p.op("scalar", lambda e, s=s, i=i, b=b, off=off: e.activation(
                    bsub(s, i * OSL, (i + 1) * OSL), bank(b, off, OSL), AF.Copy,
                    accum_out=st(s, ST_S4P + i)), "sa", 1)
                t.aw(self.Rbb(s), "sa", sa)
                p.wait("vector", "sa", sa)
                t.bw("vector", self.Rdump2())
                svq = p.op("vector", lambda e, s=s, i=i: e.scalar_tensor_tensor(
                    self.wstg(3 * DM + 2048, 3 * DM + 2048 + OSL),
                    bsub(s, i * OSL, (i + 1) * OSL), 1.0,
                    bsub(s, i * OSL, (i + 1) * OSL), ALU.mult, ALU.mult,
                    accum_out=st(s, ST_SSQ4P + i)), "sv", 1)
                t.aw(self.Rdump2(), "sv", svq)
                t.ar(self.Rps(b), "sa", sa)
                self.f_evac_sub[s] = (sa, svq)
        for s in range(NSUB):
            p.wait("vector", "sa", self.f_evac_sub[s][0])
            p.wait("vector", "sv", self.f_evac_sub[s][1])
            p.op("vector", lambda e, s=s: e.reduce_sum(
                st(s, ST_S4), st(s, ST_S4P, 16), axis=AX), "sv", 1)
            p.op("vector", lambda e, s=s: e.reduce_sum(
                st(s, ST_SSQ4), st(s, ST_SSQ4P, 16), axis=AX), "sv", 1)
            p.wait("scalar", "sv", p.ctr["sv"])
            fin4 = self.finalize_stats(s, ST_S4, ST_SSQ4, ST_M4, ST_RS4,
                                       ST_NEGMS4, 1e-5)
            p.wait("vector", "sa", fin4)
            t.bw("vector", self.Rx(s))
            t.br("vector", self.Rbb(s))
            p.op("vector", lambda e, s=s: e.tensor_scalar(
                xsub(s), xsub(s), st(s, ST_NEGMS4), None, ALU.add), "sv", 1)
            sv = p.op("vector", lambda e, s=s: e.scalar_tensor_tensor(
                xsub(s), bsub(s), st(s, ST_RS4), xsub(s), ALU.mult, ALU.add),
                "sv", 1)
            t.aw(self.Rx(s), "sv", sv)
            t.ar(self.Rbb(s), "sv", sv)
            p.wait("sync", "sv", sv)
            t.br("sync", self.Rx(s))
            ds = p.dma("sync", lambda e, g=g, s=s: e.dma_start(
                T["y_out"][(g * 512 + s * 128):(g * 512 + (s + 1) * 128), :],
                xsub(s)), f"xl{s}")
            t.ar(self.Rx(s), f"xl{s}", ds)
            self.store_ms[(g, s)] = ds
            if g + 1 < NGRP:
                t.bw("gpsimd", self.Rx(s))
                ds2 = p.dma("gpsimd", lambda e, g=g, s=s: e.dma_start(
                    xsub(s),
                    T["x_in"][((g + 1) * 512 + s * 128):
                              ((g + 1) * 512 + (s + 1) * 128), :]), f"xg{s}")
                t.aw(self.Rx(s), f"xg{s}", ds2)
                self.x0_ms[(g + 1, s)] = ds2
        if g + 1 < NGRP:
            stats_by_sub = {}
            for s in range(NSUB):
                self.a_stats(g + 1, s)
                stats_by_sub[s] = self._a_stats_ms
                if s >= 1:
                    self._a_stats_ms = stats_by_sub[s - 1]
                    self.a_fin(g + 1, s - 1)
            self._a_stats_ms = stats_by_sub[NSUB - 1]
            self.a_fin(g + 1, NSUB - 1)


_NC_CACHE = [None]


def kernel(L0_attn_hidden_states, L0_attn_Wq, L0_attn_bq, L0_attn_Wk,
           L0_attn_bk, L0_attn_Wv, L0_attn_bv, L0_attn_Wo, L0_attn_M,
           L0_attn_z, L0_ffn_lora_A, L0_ffn_lora_B, L0_ffn_W, L0_ffn_bias,
           L0_ffn_lora_weights):
    x = np.ascontiguousarray(
        np.asarray(L0_attn_hidden_states, np.float32).reshape(8192, DM))
    Wq = np.asarray(L0_attn_Wq, np.float32)
    sel = np.concatenate(
        [Wq[4 * j * 128:(4 * j + 1) * 128, :] for j in range(8)], axis=0)
    wqt_h = np.ascontiguousarray(sel.T)
    wot_h = np.ascontiguousarray(np.asarray(L0_attn_Wo, np.float32).T)
    wt_h = np.ascontiguousarray(np.asarray(L0_ffn_W, np.float32).T)
    bq_sel = np.ascontiguousarray(np.concatenate(
        [np.asarray(L0_attn_bq, np.float32)[4 * j * 128:(4 * j + 1) * 128]
         for j in range(8)]))
    A_all = np.asarray(L0_ffn_lora_A, np.float32).reshape(128, DM)
    at_hh = np.ascontiguousarray(A_all.T)
    bt_hh = np.ascontiguousarray(
        np.asarray(L0_ffn_lora_B, np.float32).transpose(0, 2, 1).reshape(128, DM))
    eye = np.eye(P, dtype=np.float32)
    e8 = np.repeat(np.eye(8, dtype=np.float32), 16, axis=1)

    if _NC_CACHE[0] is None:
        _NC_CACHE[0] = build_nc()
    nc = _NC_CACHE[0]

    in_maps = []
    for c in range(NCORE):
        in_maps.append({
            "x": np.ascontiguousarray(x[c * TOK:(c + 1) * TOK, :]),
            "wqt": wqt_h, "wot": wot_h, "wt": wt_h,
            "at_h": at_hh, "bt_h": bt_hh,
            "m_in": np.ascontiguousarray(np.asarray(L0_attn_M, np.float32)),
            "z_in": np.ascontiguousarray(np.asarray(L0_attn_z, np.float32)),
            "bq_in": bq_sel,
            "bias_in": np.ascontiguousarray(np.asarray(L0_ffn_bias, np.float32)),
            "lw_in": np.ascontiguousarray(
                np.asarray(L0_ffn_lora_weights, np.float32)),
            "eye_in": eye, "e8_in": e8,
        })
    res = run_bass_kernel_spmd(nc, in_maps, core_ids=list(range(NCORE)))
    out = np.concatenate([np.asarray(r["y"]) for r in res.results], axis=0)
    return out.reshape(1, 8192, DM)
